# version 16
# baseline (speedup 1.0000x reference)
import contextlib
import math
import numpy as np
import concourse.bass as bass
import concourse.mybir as mybir
from concourse.bass_utils import run_bass_kernel_spmd

F32 = mybir.dt.float32
BF16 = mybir.dt.bfloat16
AF = mybir.ActivationFunctionType
ALU = mybir.AluOpType
AX = mybir.AxisListType

NCORES = 8
SEQ = 2
N = 2048
D = 1024
FF = 4096
LN8 = math.log(8.0)
TT1 = 256
NT1 = N // TT1
ENGINES = ("sync", "gpsimd", "scalar", "vector", "tensor")


class Prog:

    def __init__(self, nc):
        self.nc = nc
        self.items = {e: [] for e in ENGINES}
        self.sems = {}
        self.pending = {e: [] for e in ENGINES}

    def sem(self, name):
        self.sems[name] = [self.nc.alloc_semaphore(name), 0]

    def val(self, name):
        return self.sems[name][1]

    def emit(self, eng, fn, waits=(), inc=None, incn=1):
        wl = []
        for (s, v) in waits:
            assert v <= self.sems[s][1], (s, v, self.sems[s][1])
            if v > 0:
                wl.append((self.sems[s][0], v))
        if self.pending[eng]:
            wl = self.pending[eng] + wl
            self.pending[eng] = []
        handle = None
        if inc is not None:
            self.sems[inc][1] += incn
            handle = self.sems[inc][0]
        self.items[eng].append((fn, wl, handle, incn))

    def barrier(self):
        snap = [(self.sems[s][0], self.sems[s][1]) for s in self.sems
                if self.sems[s][1] > 0]
        for e in ENGINES:
            self.pending[e] = self.pending[e] + list(snap)

    def build(self):
        with self.nc.Block() as blk:
            for engname in ENGINES:
                items = self.items[engname]
                tail = self.pending[engname]
                if not items and not tail:
                    continue

                def body(eng, items=items, tail=tail):
                    for fn, wl, handle, incn in items:
                        for h, v in wl:
                            eng.wait_ge(h, v)
                        instr = fn(eng)
                        if handle is not None:
                            instr.then_inc(handle, incn)
                    for h, v in tail:
                        eng.wait_ge(h, v)

                getattr(blk, engname)(body)
        self.items = {e: [] for e in ENGINES}
        self.pending = {e: [] for e in ENGINES}


def bcol(h):
    return (h // 8) * 8 + (h % 2) * 4 + (h % 8) // 2


DEBUG = False
STAGE = 99


def build_nc():
    nc = bass.Bass("TRN2", target_bir_lowering=False,
                   detect_race_conditions=False)

    x_d = nc.dram_tensor("x", [SEQ, N, D], F32, kind="ExternalInput")
    wT_d = {k: nc.dram_tensor(k, [D, D], F32, kind="ExternalInput")
            for k in ("wqT", "wkT", "wvT", "woT")}
    ombd_d = nc.dram_tensor("ombd", [128, 128], F32, kind="ExternalInput")
    w1T_d = nc.dram_tensor("w1T", [D, FF], F32, kind="ExternalInput")
    w2T_d = nc.dram_tensor("w2T", [FF, D], F32, kind="ExternalInput")
    b1c_d = nc.dram_tensor("b1c", [128, 32], F32, kind="ExternalInput")
    b2c_d = nc.dram_tensor("b2c", [128, 8], F32, kind="ExternalInput")
    bo_d = nc.dram_tensor("bo_row", [1, D], F32, kind="ExternalInput")
    ident_d = nc.dram_tensor("ident", [128, 128], F32, kind="ExternalInput")
    mask_d = nc.dram_tensor("mask4", [128, 512], F32, kind="ExternalInput")
    seln_d = nc.dram_tensor("seln", [128, 128], F32, kind="ExternalInput")
    hsel_d = nc.dram_tensor("halfsel", [128, 128], F32,
                            kind="ExternalInput")
    shF_d = nc.dram_tensor("selhlF", [32, 1024], F32, kind="ExternalInput")
    shT_d = nc.dram_tensor("selhlT", [32, 1024], F32, kind="ExternalInput")
    xnT_s = nc.dram_tensor("xnT_s", [SEQ, 8, 128, N], BF16, kind="Internal")
    g_s = nc.dram_tensor("g_s", [SEQ, N, D], F32, kind="Internal")
    bohl_scr = nc.dram_tensor("bohl_scr", [2, 1024], BF16, kind="Internal")
    y_d = nc.dram_tensor("y", [SEQ, N, D], F32, kind="ExternalOutput")
    if DEBUG:
        dbg_xnt = nc.dram_tensor("dbg_xnt", [SEQ, 8, 128, N], BF16,
                                 kind="ExternalOutput")
        dbg_g = nc.dram_tensor("dbg_g", [SEQ, N, D], F32,
                               kind="ExternalOutput")
        dbg_pt = nc.dram_tensor("dbg_pt", [8, 128, 4096], F32,
                                kind="ExternalOutput")
        dbg_w1 = nc.dram_tensor("dbg_w1", [128, 8 * FF], BF16,
                                kind="ExternalOutput")
        dbg_w2 = nc.dram_tensor("dbg_w2", [128, 32 * D], BF16,
                                kind="ExternalOutput")
        dbg_h = nc.dram_tensor("dbg_h", [128, 2 * 256], BF16,
                               kind="ExternalOutput")
        dbg_nr = nc.dram_tensor("dbg_nr", [32, 512], F32,
                                kind="ExternalOutput")
        dbg_ss = nc.dram_tensor("dbg_ss", [32, 64, 1040], F32,
                                kind="ExternalOutput")

    A = nc.alloc_sbuf_tensor
    ident_sb = A("ident_sb", [128, 128], F32)
    mask_sb = A("mask_sb", [128, 512], BF16)
    seln_sb = A("seln_sb", [128, 128], F32)
    hsel_sb = A("hsel_sb", [128, 128], F32)
    shF_sb = A("shF_sb", [32, 1024], BF16)
    shT_sb = A("shT_sb", [32, 1024], BF16)
    ombd_sb = A("ombd_sb", [128, 128], BF16)
    b1c_sb = A("b1c_sb", [128, 32], F32)
    b2c_sb = A("b2c_sb", [128, 8], F32)
    bohl_sb = A("bohl_sb", [2, D], BF16)
    ones2_sb = A("ones2_sb", [2, 128], BF16)
    onesc_sb = A("onesc_sb", [128, 1], F32)
    onesr_sb = A("onesr_sb", [1, 128], F32)
    negl8_sb = A("negl8_sb", [128, 1], F32)
    eps_sb = A("eps_sb", [128, 1], F32)

    PB = [nc.alloc_psum_tensor(f"PB{i}", [128, 512], F32) for i in range(8)]

    prog = Prog(nc)
    for s in ("SX", "SY", "SC", "SG", "PE", "DV", "AC"):
        prog.sem(s)
    E = prog.emit
    V = prog.val

    for dst, src in ((ident_sb, ident_d), (seln_sb, seln_d),
                     (b1c_sb, b1c_d), (b2c_sb, b2c_d),
                     (hsel_sb, hsel_d)):
        E("sync", lambda e, d=dst, s=src: e.dma_start(d[:], s.ap()),
          inc="SX", incn=16)
    sx_setup = V("SX")
    for dst, src in ((ombd_sb, ombd_d), (shF_sb, shF_d), (shT_sb, shT_d),
                     (mask_sb, mask_d)):
        E("gpsimd", lambda e, d=dst, s=src: e.dma_start(d[:], s.ap()),
          inc="SG", incn=16)
    sg_setup = V("SG")
    E("vector", lambda e: e.memset(ones2_sb[:], 1.0), inc="DV")
    E("vector", lambda e: e.memset(onesc_sb[:], 1.0), inc="DV")
    E("vector", lambda e: e.memset(onesr_sb[:], 1.0), inc="DV")
    E("vector", lambda e: e.memset(negl8_sb[:], -LN8), inc="DV")
    E("vector", lambda e: e.memset(eps_sb[:], 1e-5), inc="DV")
    dv_ones = V("DV")

    with contextlib.ExitStack() as st1:
        S1 = lambda name, shape, dt: st1.enter_context(
            nc.sbuf_tensor(name, shape, dt))
        bor_sb = S1("bor_sb", [128, 8], F32)
        bohi_b = S1("bohi_b", [128, 8], BF16)
        bohi_f = S1("bohi_f", [128, 8], F32)
        bolo_f = S1("bolo_f", [128, 8], F32)
        bolo_b = S1("bolo_b", [128, 8], BF16)
        w1bf = S1("w1bf", [128, 8 * FF], BF16)
        w2bf = S1("w2bf", [128, 32 * D], BF16)
        wst = S1("wst", [128, 2 * 1024], F32)
        wst2 = S1("wst2", [128, 2 * 1024], F32)
        tn = S1("tn", [128, 1024], BF16)
        tn2 = S1("tn2", [128, 1024], BF16)
        acc = S1("acc", [128, 4], F32)
        acc2 = S1("acc2", [128, 4], F32)
        inv1 = S1("inv1", [128, 1], F32)
        inv2 = S1("inv2", [128, 1], F32)
        sc1 = S1("sc1", [1, 4], F32)
        x1 = S1("x1", [128, 3 * 1024], F32)
        scr = S1("scr", [128, 2 * 1024], F32)
        lnc = S1("lnc", [128, 16], F32)
        xnT1 = S1("xnT1", [128, 2 * 2048], BF16)
        hbf = S1("hbf", [128, 2 * 256], BF16)
        gTr = S1("gTr", [128, 2 * 256], F32)
        gtok = S1("gtok", [128, 2048], F32)

        E("sync", lambda e: e.dma_start(
            bor_sb[:], bo_d.ap().rearrange("o (p c) -> (o p) c", p=128)),
          inc="SX", incn=16)
        sx_bo = V("SX")
        E("vector", lambda e: e.tensor_copy(bohi_b[:], bor_sb[:]),
          waits=[("SX", sx_bo), ("DV", dv_ones)], inc="DV")
        E("vector", lambda e: e.tensor_copy(bohi_f[:], bohi_b[:]),
          waits=[("DV", V("DV"))], inc="DV")
        E("vector", lambda e: e.tensor_tensor(bolo_f[:], bor_sb[:],
                                              bohi_f[:], ALU.subtract),
          waits=[("DV", V("DV"))], inc="DV")
        E("vector", lambda e: e.tensor_copy(bolo_b[:], bolo_f[:]),
          waits=[("DV", V("DV"))], inc="DV")
        dv_bolo = V("DV")
        E("sync", lambda e: e.dma_start(
            bohl_scr.ap()[0:1, :].rearrange("o (p c) -> (o p) c", p=128),
            bohi_b[:]),
          waits=[("DV", dv_bolo)], inc="SX", incn=16)
        E("sync", lambda e: e.dma_start(
            bohl_scr.ap()[1:2, :].rearrange("o (p c) -> (o p) c", p=128),
            bolo_b[:]),
          waits=[("DV", dv_bolo)], inc="SX", incn=16)
        sx_bscr = V("SX")
        E("sync", lambda e: e.dma_start(bohl_sb[:], bohl_scr.ap()),
          waits=[("SX", sx_bscr)], inc="SX", incn=16)

        mats = [
            dict(mat=w1T_d, nco=FF, dst=w1bf, inv=inv1, stg=wst,
                 accb=acc, tnb=tn, bank=0, scc=0, per=4, dq="SX"),
            dict(mat=w2T_d, nco=D, dst=w2bf, inv=inv2, stg=wst2,
                 accb=acc2, tnb=tn2, bank=1, scc=2, per=1, dq="SY"),
        ]
        for m in mats:
            E("vector", lambda e, m=m: e.memset(m["accb"][:, 1:2], 0.0),
              inc="DV")
        dv_ms = V("DV")
        tguard = [[dv_ms, dv_ms], [dv_ms, dv_ms]]
        for k in range(32):
            for i, m in enumerate(mats):
                rk, ck = k // m["per"], k % m["per"]
                slot = k % 2
                src = m["mat"].ap()[rk * 128:(rk + 1) * 128,
                                    ck * 1024:(ck + 1) * 1024]
                E("sync", lambda e, s=src, m=m, sl=slot: e.dma_start(
                    m["stg"][:, sl * 1024:(sl + 1) * 1024], s),
                  waits=[("DV", tguard[i][slot])], inc=m["dq"], incn=16)
                sxv = V(m["dq"])
                E("vector", lambda e, m=m, sl=slot: e.reduce_sum(
                    m["accb"][:, 2:3],
                    m["stg"][:, sl * 1024:(sl + 1) * 1024],
                    axis=AX.X, apply_absolute_value=True),
                  waits=[(m["dq"], sxv)], inc="DV")
                dv_red = V("DV")
                E("vector", lambda e, m=m, k=k: e.tensor_add(
                    m["accb"][:, k % 2:k % 2 + 1],
                    m["accb"][:, (k + 1) % 2:(k + 1) % 2 + 1],
                    m["accb"][:, 2:3]),
                  waits=[("DV", dv_red)], inc="DV")
                tguard[i][slot] = V("DV")
        dv_invc = [0, 0]
        for i, m in enumerate(mats):
            c_acc = tguard[i][1 - 32 % 2]
            E("tensor", lambda e, m=m: e.matmul(
                PB[m["bank"]][0:1, 0:1], m["accb"][:, 1:2], onesc_sb[:],
                start=True, stop=True, skip_group_check=True),
              waits=[("DV", c_acc)], inc="PE")
            pe_tot = V("PE")
            E("vector", lambda e, m=m: e.tensor_scalar(
                sc1[:, m["scc"]:m["scc"] + 1], PB[m["bank"]][0:1, 0:1],
                1.0 / (D * FF), 1e-8, ALU.mult, ALU.max),
              waits=[("PE", pe_tot)], inc="DV")
            E("vector", lambda e, m=m: e.reciprocal(
                sc1[:, m["scc"] + 1:m["scc"] + 2],
                sc1[:, m["scc"]:m["scc"] + 1]),
              waits=[("DV", V("DV"))], inc="DV")
            dv_inv = V("DV")
            E("tensor", lambda e, m=m: e.matmul(
                PB[m["bank"]][:, 0:1], onesr_sb[:],
                sc1[:, m["scc"] + 1:m["scc"] + 2],
                start=True, stop=True, skip_group_check=True),
              waits=[("DV", dv_inv)], inc="PE")
            pe_bc = V("PE")
            E("vector", lambda e, m=m: e.tensor_copy(
                m["inv"][:], PB[m["bank"]][:, 0:1]),
              waits=[("PE", pe_bc)], inc="DV")
            dv_invc[i] = V("DV")
        tguard = [[V("DV")] * 2, [V("DV")] * 2]
        tdone = [0, 0]
        for k in range(32):
            for i, m in enumerate(mats):
                rk, ck = k // m["per"], k % m["per"]
                slot = k % 2
                src = m["mat"].ap()[rk * 128:(rk + 1) * 128,
                                    ck * 1024:(ck + 1) * 1024]
                E("sync", lambda e, s=src, m=m, sl=slot: e.dma_start(
                    m["stg"][:, sl * 1024:(sl + 1) * 1024], s),
                  waits=[("DV", tguard[i][slot])], inc=m["dq"], incn=16)
                sxv = V(m["dq"])
                d0 = rk * m["nco"] + ck * 1024
                E("vector", lambda e, m=m, sl=slot, d0=d0: e.tensor_scalar(
                    m["dst"][:, d0:d0 + 1024],
                    m["stg"][:, sl * 1024:(sl + 1) * 1024],
                    m["inv"][:, 0:1], 0.5, ALU.mult, ALU.is_gt),
                  waits=[(m["dq"], sxv), ("DV", dv_invc[i])])
                E("vector", lambda e, m=m, sl=slot: e.tensor_scalar(
                    m["tnb"][:], m["stg"][:, sl * 1024:(sl + 1) * 1024],
                    m["inv"][:, 0:1], -0.5, ALU.mult, ALU.is_lt),
                  inc="DV")
                tguard[i][slot] = V("DV")
                E("vector", lambda e, m=m, d0=d0: e.tensor_tensor(
                    m["dst"][:, d0:d0 + 1024], m["dst"][:, d0:d0 + 1024],
                    m["tnb"][:], ALU.subtract),
                  waits=[("DV", V("DV"))], inc="DV")
                tdone[i] = V("DV")
        tern1_dv, tern2_sg = tdone[0], tdone[1]
        if DEBUG:
            E("sync", lambda e: e.dma_start(dbg_w1.ap(), w1bf[:]),
              waits=[("DV", tern1_dv)], inc="SX", incn=16)
            E("sync", lambda e: e.dma_start(dbg_w2.ap(), w2bf[:]),
              waits=[("DV", tern2_sg)], inc="SX", incn=16)

        xg = [0, 0, 0]
        scr_g = [0, 0]
        pb01f = [0, 0]
        out2_free = 0
        pb6_ev = 0
        pb7_ev = 0
        xnt_rd = [0, 0]
        xnt_use = [0, 0]
        gtok_st = 0

        def emit_ffn(s, t0, slot, ln_mid=None):
            nonlocal out2_free, gtok_st, pb7_ev
            pe_mm2_last = 0
            ac_hv = [0, 0]
            h_free = [0, 0]
            for F in range(33):
                if F < 32:
                    b = F % 2
                    for Dc in range(8):
                        w = []
                        if Dc == 0:
                            w = [("DV", tern1_dv), ("DV", xnt_rd[slot]),
                                 ("AC", pb01f[b])]
                            if F == 0:
                                w += [("AC", out2_free),
                                      ("DV", tern2_sg)]
                        E("tensor", lambda e, F=F, Dc=Dc, b=b, slot=slot:
                          e.matmul(PB[b][:, 0:256],
                                   w1bf[:, Dc * FF + F * 128:
                                        Dc * FF + (F + 1) * 128],
                                   xnT1[:, slot * 2048 + Dc * 256:
                                        slot * 2048 + (Dc + 1) * 256],
                                   start=(Dc == 0), stop=(Dc == 7),
                                   skip_group_check=True),
                          waits=w, inc=("PE" if Dc == 7 else None))
                    pe_mm1 = V("PE")
                if F >= 1:
                    Fm = F - 1
                    hs2 = Fm % 2
                    for Ec in range(8):
                        ob = 2 + Ec // 2
                        oc = (Ec % 2) * 256
                        w = [("AC", ac_hv[hs2])] if Ec == 0 else []
                        E("tensor", lambda e, Fm=Fm, Ec=Ec, ob=ob, oc=oc,
                          hs2=hs2: e.matmul(
                              PB[ob][:, oc:oc + 256],
                              w2bf[:, Fm * D + Ec * 128:
                                   Fm * D + (Ec + 1) * 128],
                              hbf[:, hs2 * 256:(hs2 + 1) * 256],
                              start=(Fm == 0 and Ec % 2 == 0),
                              stop=(Fm == 31), skip_group_check=True),
                          waits=w,
                          inc=("PE" if (Ec == 7 or Fm == 31) else None))
                    pe_mm2_last = V("PE")
                    h_free[hs2] = pe_mm2_last
                if F < 32:
                    hs = F % 2
                    E("scalar", lambda e, F=F, b=b, hs=hs: e.activation(
                        hbf[:, hs * 256:(hs + 1) * 256], PB[b][:, 0:256],
                        AF.Gelu, bias=b1c_sb[:, F:F + 1]),
                      waits=[("PE", pe_mm1), ("PE", h_free[hs])],
                      inc="AC")
                    pb01f[b] = V("AC")
                    ac_hv[hs] = V("AC")
                if F == 8 and ln_mid is not None:
                    ln_mid()
            gtr_free = [0, 0]
            for Ec in range(8):
                ob = 2 + Ec // 2
                oc = (Ec % 2) * 256
                gs = Ec % 2
                E("scalar", lambda e, ob=ob, oc=oc, gs=gs, Ec=Ec:
                  e.activation(gTr[:, gs * 256:(gs + 1) * 256],
                               PB[ob][:, oc:oc + 256], AF.Silu,
                               bias=b2c_sb[:, Ec:Ec + 1]),
                  waits=[("PE", pe_mm2_last), ("PE", gtr_free[gs])],
                  inc="AC")
                ac_g = V("AC")
                for ts in range(2):
                    w = ([("AC", ac_g), ("DV", pb7_ev)] if ts == 0 else [])
                    E("tensor", lambda e, ts=ts, gs=gs: e.transpose(
                        PB[7][:, gs * 256 + ts * 128:
                              gs * 256 + (ts + 1) * 128],
                        gTr[:, gs * 256 + ts * 128:
                            gs * 256 + (ts + 1) * 128],
                        ident_sb[:]),
                      waits=w, inc="PE")
                gtr_free[gs] = V("PE")
                if Ec % 2 == 1:
                    pe_gt = V("PE")
                    e0 = Ec - 1
                    dst = gtok.ap().rearrange(
                        "p (ts ec c) -> p ec ts c", ts=2, ec=8)[
                        :, e0:e0 + 2, :, :]
                    src = PB[7].ap().rearrange(
                        "p (ec ts c) -> p ec ts c", ec=2, ts=2)
                    E("vector", lambda e, dst=dst, src=src:
                      e.tensor_copy(dst, src),
                      waits=[("PE", pe_gt), ("SC", gtok_st)], inc="DV")
                    pb7_ev = V("DV")
            if DEBUG and s == SEQ - 1 and t0 == N - TT1:
                E("sync", lambda e: e.dma_start(dbg_h.ap(), hbf[:]),
                  waits=[("PE", pe_mm2_last)], inc="SX", incn=16)
            out2_free = V("AC")
            dv_g = V("DV")
            E("scalar", lambda e, s=s, t0=t0: e.dma_start(
                g_s.ap()[s, t0:t0 + TT1, :].rearrange(
                    "(ts p) e -> p ts e", p=128),
                gtok.ap().rearrange("p (ts e) -> p ts e", ts=2)),
              waits=[("DV", dv_g)], inc="SC", incn=16)
            gtok_st = V("SC")
            xnt_use[slot] = pe_mm2_last

        def ln_chain(it, box):
            dv_xn = []
            for ts in range(2):
                xs = (2 * it + ts) % 3
                sxv = box["sxv"][ts]
                c = ts * 8
                E("scalar", lambda e, xs=xs, ts=ts, c=c: e.activation(
                    scr[:, ts * 1024:(ts + 1) * 1024],
                    x1[:, xs * 1024:(xs + 1) * 1024], AF.Square,
                    accum_out=lnc[:, c:c + 1]),
                  waits=[sxv, ("PE", scr_g[ts])], inc="AC")
                E("scalar", lambda e, xs=xs, ts=ts, c=c: e.activation(
                    scr[:, ts * 1024:(ts + 1) * 1024],
                    x1[:, xs * 1024:(xs + 1) * 1024], AF.Copy,
                    accum_out=lnc[:, c + 1:c + 2]),
                  inc="AC")
                ac_sums = V("AC")
                E("gpsimd", lambda e, c=c: e.tensor_scalar(
                    lnc[:, c + 2:c + 3], lnc[:, c + 1:c + 2], 1.0 / D,
                    None, ALU.mult),
                  waits=[("AC", ac_sums)], inc="SG")
                E("gpsimd", lambda e, c=c: e.tensor_tensor(
                    lnc[:, c + 3:c + 4], lnc[:, c + 2:c + 3],
                    lnc[:, c + 2:c + 3], ALU.mult),
                  inc="SG")
                E("gpsimd", lambda e, c=c: e.tensor_scalar(
                    lnc[:, c + 4:c + 5], lnc[:, c:c + 1], 1.0 / D,
                    None, ALU.mult),
                  inc="SG")
                E("gpsimd", lambda e, c=c: e.tensor_tensor(
                    lnc[:, c + 4:c + 5], lnc[:, c + 4:c + 5],
                    lnc[:, c + 3:c + 4], ALU.subtract),
                  inc="SG")
                sg_var = V("SG")
                E("scalar", lambda e, c=c: e.activation(
                    lnc[:, c + 5:c + 6], lnc[:, c + 4:c + 5], AF.Ln,
                    bias=eps_sb[:, 0:1]),
                  waits=[("SG", sg_var), ("DV", dv_ones)], inc="AC")
                E("scalar", lambda e, c=c: e.activation(
                    lnc[:, c + 6:c + 7], lnc[:, c + 5:c + 6], AF.Exp,
                    scale=-0.5),
                  waits=[("AC", V("AC"))], inc="AC")
                ac_rstd = V("AC")
                E("gpsimd", lambda e, c=c: e.tensor_tensor(
                    lnc[:, c + 7:c + 8], lnc[:, c + 2:c + 3],
                    lnc[:, c + 6:c + 7], ALU.mult),
                  waits=[("AC", ac_rstd)], inc="SG")
                E("gpsimd", lambda e, c=c: e.tensor_scalar(
                    lnc[:, c + 7:c + 8], lnc[:, c + 7:c + 8], -1.0,
                    None, ALU.mult),
                  inc="SG")
                sg_mb = V("SG")
                E("scalar", lambda e, xs=xs, ts=ts, c=c: e.activation(
                    scr[:, ts * 1024:(ts + 1) * 1024],
                    x1[:, xs * 1024:(xs + 1) * 1024], AF.Identity,
                    scale=lnc[:, c + 6:c + 7],
                    bias=lnc[:, c + 7:c + 8]),
                  waits=[("SG", sg_mb)], inc="AC")
                dv_xn.append(("AC", V("AC")))
                xg[xs] = V("AC")
            box["dv_xn"] = dv_xn

        for it in range(SEQ * NT1):
            s, T = it // NT1, it % NT1
            t0 = T * TT1
            slot = it % 2
            box = {"sxv": []}
            for ts in range(2):
                xs = (2 * it + ts) % 3
                xq = "SX" if ts == 0 else "SY"
                xap = x_d.ap()[s, t0 + ts * 128:t0 + (ts + 1) * 128, :]
                E("sync", lambda e, xap=xap, xs=xs: e.dma_start(
                    x1[:, xs * 1024:(xs + 1) * 1024], xap),
                  waits=[("AC", xg[xs])], inc=xq, incn=16)
                box["sxv"].append((xq, V(xq)))
            if it == 0:
                ln_chain(0, box)
            else:
                emit_ffn(*ffn_prev,
                         ln_mid=lambda it=it, box=box: ln_chain(it, box))
            dv_xn = box["dv_xn"]
            for ts in range(2):
                for wave in range(2):
                    for j in range(4):
                        cidx = wave * 4 + j
                        w = ([dv_xn[ts], ("DV", pb6_ev),
                              ("SX", sx_setup)] if j == 0 else [])
                        E("tensor", lambda e, ts=ts, cidx=cidx, j=j:
                          e.transpose(PB[6][:, j * 128:(j + 1) * 128],
                                      scr[:, ts * 1024 + cidx * 128:
                                          ts * 1024 + (cidx + 1) * 128],
                                      ident_sb[:]),
                          waits=w, inc="PE")
                    pe_tp = V("PE")
                    dst = xnT1.ap()[:, slot * 2048:(slot + 1) * 2048]
                    dst = dst.rearrange("p (dc t) -> p dc t", dc=8)[
                        :, wave * 4:(wave + 1) * 4,
                        ts * 128:(ts + 1) * 128]
                    src4 = PB[6].ap().rearrange("p (dc t) -> p dc t", dc=4)
                    E("vector", lambda e, dst=dst, src4=src4:
                      e.tensor_copy(dst, src4),
                      waits=[("PE", pe_tp), ("PE", xnt_use[slot])],
                      inc="DV")
                    pb6_ev = V("DV")
            scr_g[0] = scr_g[1] = V("PE")
            xnt_rd[slot] = V("DV")
            E("scalar", lambda e, s=s, t0=t0, slot=slot: e.dma_start(
                xnT_s.ap()[s].rearrange("c p t -> p c t")[
                    :, :, t0:t0 + TT1],
                xnT1.ap()[:, slot * 2048:(slot + 1) * 2048].rearrange(
                    "p (c t) -> p c t", c=8)),
              waits=[("DV", xnt_rd[slot])], inc="SC", incn=16)
            ffn_prev = (s, t0, slot)

        emit_ffn(*ffn_prev)
        prog.build()

    prog.barrier()
    if STAGE <= 1:
        prog.build()
        return nc
    with contextlib.ExitStack() as st2:
        S2 = lambda name, shape, dt: st2.enter_context(
            nc.sbuf_tensor(name, shape, dt))
        wbf = {k: S2(k + "bf", [128, 8 * 1024], BF16)
               for k in ("wq", "wk", "wv", "wo")}
        xnT2 = S2("xnT2", [128, 8 * 512], BF16)
        qT = S2("qT", [128, 8 * 512], BF16)
        kT = S2("kT", [128, 8 * 512], BF16)
        kTo = S2("kTo", [64, 8 * 512], BF16)
        sqp = S2("sqp", [128, 2 * 512], F32)
        nhb = S2("nhb", [128, 16], BF16)
        nhl = S2("nhl", [128, 32], F32)
        nrTk = S2("nrTk", [32, 512], BF16)
        phiqT = S2("phiqT", [128, 8 * 512], BF16)
        phiqTo = S2("phiqTo", [64, 8 * 512], BF16)
        phikT = S2("phikT", [128, 8 * 512], BF16)
        phikTo = S2("phikTo", [64, 8 * 512], BF16)
        phikt = S2("phikt", [128, 4 * 1024], BF16)
        vp = S2("vp", [128, 4 * 1040], BF16)
        Ssb = S2("Ssb", [64, 16 * 65], BF16)
        kvb = S2("kvb", [64, 2 * 1024], F32)
        dnb = S2("dnb", [64, 2 * 16], F32)
        BT = S2("BT", [128, 16 * 128], BF16)
        zr = S2("zr", [128, 8], F32)
        attn = S2("attn", [128, 1024], F32)
        attnT = S2("attnT", [128, 4 * 1024], BF16)
        x2 = S2("x2", [128, 2 * 1024], F32)
        g2 = S2("g2", [128, 2 * 1024], F32)
        sg = S2("sg", [128, 512], F32)
        outb = S2("outb", [128, 1024], F32)

        for k, src in (("wq", wT_d["wqT"]), ("wk", wT_d["wkT"]),
                       ("wv", wT_d["wvT"]), ("wo", wT_d["woT"])):
            E("gpsimd", lambda e, k=k, src=src: e.dma_start(
                wbf[k].ap().rearrange("p (c o) -> p c o", c=8),
                src.ap().rearrange("(c p) o -> p c o", p=128)),
              inc="SG", incn=16)
        sg_w = V("SG")
        E("vector", lambda e: e.memset(vp[:], 1.0), inc="DV")
        dv_vp0 = V("DV")

        xnT2_g = 0
        pb01f = [0, 0]
        pb23f = [("DV", 0), ("DV", 0)]
        zr_free = [0, 0]
        attn_free = 0
        pb3f = 0
        pb6f = 0
        pb7f = 0
        pb45f = 0
        sq_g = [0, 0]
        nhl_free = 0
        x2f = [0, 0]
        g2f = [0, 0]
        outb_st = [0]
        ssb_ev = 0

        for s in range(SEQ):
            E("vector", lambda e: e.memset(kvb[0:64, 0:1024], 0.0), inc="DV")
            E("vector", lambda e: e.memset(dnb[0:64, 0:16], 0.0), inc="DV")
            for qtr in range(4):
                pb23f = [("DV", V("DV")), ("DV", V("DV"))]
                pb45f = V("DV")
                pb2f = V("DV")
                dvsnap = V("DV")
                for T in range(1):
                    tg0 = qtr * 512
                    E("sync", lambda e, s=s, tg0=tg0: e.dma_start(
                        xnT2.ap().rearrange("p (c t) -> p c t", c=8),
                        xnT_s.ap()[s].rearrange("c p t -> p c t")[
                            :, :, tg0:tg0 + 512]),
                      waits=[("PE", xnT2_g)], inc="SX", incn=16)
                    sx_x = V("SX")
                    hilo_dv = [0, 0]
                    dv_scan = V("DV")
                    phb_free = [0, 0]
                    for idx, wname in ((0, "wq"), (1, "wk")):
                        qk = qT if idx == 0 else kT
                        prev = None
                        for p in range(9):
                            if p < 8:
                                for Dc in range(8):
                                    w = []
                                    if Dc == 0:
                                        w = [("SX", sx_x), ("SG", sg_w),
                                             ("AC", pb01f[p % 2])]
                                        if p == 0 and idx == 0 and T == 0:
                                            w.append(("DV", dvsnap))
                                    E("tensor", lambda e, wname=wname, p=p,
                                      Dc=Dc: e.matmul(
                                          PB[p % 2][:, 0:512],
                                          wbf[wname][:,
                                                     Dc * 1024 + p * 128:
                                                     Dc * 1024 +
                                                     (p + 1) * 128],
                                          xnT2[:, Dc * 512:(Dc + 1) * 512],
                                          start=(Dc == 0), stop=(Dc == 7),
                                          skip_group_check=True),
                                      waits=w,
                                      inc=("PE" if Dc == 7 else None))
                                pe_q = V("PE")
                            if p >= 1:
                                pm = p - 1
                                sl = pm % 2
                                dv_sq, ac_qkm = prev
                                if idx == 0:
                                    w = [("SG", sg_setup), ("AC", ac_qkm),
                                         ("AC", phb_free[pm % 2])]
                                    if pm < 2:
                                        w.append(("DV", dv_scan))
                                    E("tensor", lambda e, pm=pm: e.matmul(
                                        PB[2 + pm % 2][:, 0:512],
                                        ombd_sb[:],
                                        qT[:, pm * 512:(pm + 1) * 512],
                                        start=True, stop=False,
                                        skip_group_check=True),
                                      waits=w)
                                    E("tensor", lambda e, pm=pm, sl=sl:
                                      e.matmul(
                                          PB[2 + pm % 2][:, 0:512],
                                          hsel_sb[:],
                                          sqp[:, sl * 512:(sl + 1) * 512],
                                          start=False, stop=True,
                                          skip_group_check=True),
                                      waits=[("DV", dv_sq),
                                             ("SX", sx_setup)],
                                      inc="PE")
                                    sq_g[sl] = V("PE")
                                    pe_phi = V("PE")
                                    E("scalar", lambda e, pm=pm:
                                      e.activation(
                                          phiqT[:, pm * 512:
                                                (pm + 1) * 512],
                                          PB[2 + pm % 2][:, 0:512],
                                          AF.Exp,
                                          bias=negl8_sb[:, 0:1]),
                                      waits=[("PE", pe_phi),
                                             ("DV", dv_ones)],
                                      inc="AC")
                                    E("scalar", lambda e, pm=pm:
                                      e.activation(
                                          phiqTo[:, pm * 512:
                                                 (pm + 1) * 512],
                                          PB[2 + pm % 2][64:128, 0:512],
                                          AF.Exp,
                                          bias=negl8_sb[0:64, 0:1]),
                                      inc="AC")
                                    phb_free[pm % 2] = V("AC")
                                else:
                                    for ts in range(4):
                                        w = [("DV", dv_sq)]
                                        if pm == 0 and ts == 0:
                                            w += [("DV", pb2f),
                                                  ("AC", phb_free[0])]
                                        E("tensor", lambda e, pm=pm, ts=ts,
                                          sl=sl: e.matmul(
                                              PB[2][:, 64 + ts * 16:
                                                    64 + (ts + 1) * 16],
                                              sqp[:, sl * 512 + ts * 128:
                                                  sl * 512 +
                                                  (ts + 1) * 128],
                                              seln_sb[:, pm * 16:
                                                      (pm + 1) * 16],
                                              start=(pm == 0 and ts == 0),
                                              stop=(pm == 7 and ts == 3),
                                              skip_group_check=True),
                                          waits=w, inc="PE")
                                    sq_g[sl] = V("PE")
                            if p < 8:
                                E("scalar", lambda e, qk=qk, p=p:
                                  e.activation(
                                      qk[:, p * 512:(p + 1) * 512],
                                      PB[p % 2][:, 0:512], AF.Copy),
                                  waits=[("PE", pe_q)], inc="AC")
                                ac_qk = V("AC")
                                if idx == 1:
                                    E("scalar", lambda e, p=p:
                                      e.activation(
                                          kTo[:, p * 512:(p + 1) * 512],
                                          PB[p % 2][64:128, 0:512],
                                          AF.Copy),
                                      inc="AC")
                                pb01f[p % 2] = V("AC")
                                E("vector", lambda e, qk=qk, p=p:
                                  e.tensor_tensor(
                                      sqp[:, (p % 2) * 512:
                                          (p % 2 + 1) * 512],
                                      qk[:, p * 512:(p + 1) * 512],
                                      qk[:, p * 512:(p + 1) * 512],
                                      ALU.mult),
                                  waits=[("AC", ac_qk),
                                         ("PE", sq_g[p % 2])],
                                  inc="DV")
                                prev = (V("DV"), ac_qk)
                        if idx == 0:
                            continue
                        pe_norm = V("PE")
                        for ts in range(4):
                            off = 64 + ts * 16
                            E("vector", lambda e, off=off: e.tensor_copy(
                                nhb[:], PB[2][:, off:off + 16]),
                              waits=[("PE", pe_norm), ("DV", V("DV")),
                                     ("PE", nhl_free)],
                              inc="DV")
                            E("vector", lambda e: e.tensor_copy(
                                nhl[:, 0:16], nhb[:]),
                              waits=[("DV", V("DV"))], inc="DV")
                            E("vector", lambda e, off=off: e.tensor_tensor(
                                nhl[:, 16:32], PB[2][:, off:off + 16],
                                nhl[:, 0:16], ALU.subtract),
                              waits=[("DV", V("DV"))], inc="DV")
                            dv_nhl = V("DV")
                            E("tensor", lambda e, ts=ts: e.transpose(
                                PB[3][0:32, ts * 128:(ts + 1) * 128],
                                nhl[:], ident_sb[:]),
                              waits=[("DV", dv_nhl), ("DV", pb3f)],
                              inc="PE")
                            pe_ntp = V("PE")
                            nhl_free = pe_ntp
                            E("vector", lambda e, ts=ts:
                              e.tensor_copy(
                                  nrTk[0:32, ts * 128:(ts + 1) * 128],
                                  PB[3][0:32, ts * 128:(ts + 1) * 128]),
                              waits=[("PE", pe_ntp)], inc="DV")
                            pb3f = V("DV")
                        hilo_dv[1] = V("DV")
                        pb2f = V("DV")
                    if STAGE < 3:
                        continue
                    for p in range(8):
                        w = [("SG", sg_setup), ("DV", hilo_dv[1]),
                             ("AC", pb01f[p % 2])]
                        E("tensor", lambda e, p=p: e.matmul(
                            PB[p % 2][:, 0:512], ombd_sb[:],
                            kT[:, p * 512:(p + 1) * 512],
                            start=True, stop=False,
                            skip_group_check=True),
                          waits=w)
                        E("tensor", lambda e, p=p: e.matmul(
                            PB[p % 2][:, 0:512],
                            shF_sb[:, p * 128:(p + 1) * 128],
                            nrTk[0:32, :], start=False, stop=True,
                            skip_group_check=True),
                          inc="PE")
                        pe_phi = V("PE")
                        E("scalar", lambda e, p=p:
                          e.activation(
                              phikT[:, p * 512:(p + 1) * 512],
                              PB[p % 2][:, 0:512], AF.Exp,
                              bias=negl8_sb[:, 0:1]),
                          waits=[("PE", pe_phi), ("DV", dv_ones)],
                          inc="AC")
                        E("scalar", lambda e, p=p:
                          e.activation(
                              phikTo[:, p * 512:(p + 1) * 512],
                              PB[p % 2][64:128, 0:512], AF.Exp,
                              bias=negl8_sb[0:64, 0:1]),
                          inc="AC")
                        pb01f[p % 2] = V("AC")
                    if STAGE < 4:
                        xnT2_g = 0
                        continue
                    for ts in range(4):
                        for oh in range(2):
                            for Dc in range(8):
                                w = []
                                if Dc == 0 and oh == 0:
                                    w = [("DV", pb45f), ("SG", sg_w)]
                                E("tensor", lambda e, ts=ts, oh=oh, Dc=Dc:
                                  e.matmul(
                                      PB[4 + oh][:, 0:512],
                                      xnT2[:, Dc * 512 + ts * 128:
                                           Dc * 512 + (ts + 1) * 128],
                                      wbf["wv"][:, Dc * 1024 + oh * 512:
                                                Dc * 1024 +
                                                (oh + 1) * 512],
                                      start=(Dc == 0), stop=(Dc == 7),
                                      skip_group_check=True),
                                  waits=w,
                                  inc=("PE" if Dc == 7 else None))
                        pe_v = V("PE")
                        cl = T * 4 + ts
                        for bank in range(2):
                            dst = vp.ap().rearrange(
                                "p (cc h j) -> p cc h j", cc=4, h=16)[
                                :, cl, bank * 8:(bank + 1) * 8, 0:64]
                            src = PB[4 + bank].ap().rearrange(
                                "p (h j) -> p h j", h=8)
                            E("vector", lambda e, dst=dst, src=src:
                              e.tensor_copy(dst, src),
                              waits=[("PE", pe_v), ("DV", dv_vp0)],
                              inc="DV")
                        pb45f = V("DV")
                    xnT2_g = V("PE")
                    if STAGE < 5:
                        continue
                    pb67_ac = V("AC")
                    for ts in range(4):
                        cl = T * 4 + ts
                        ptk_order = [0, 2, 4, 6, 8, 10, 12, 14,
                                     1, 3, 5, 7, 9, 11, 13, 15]
                        for oi, hh in enumerate(ptk_order):
                            bank = 6 + hh // 8
                            kk = kT if hh % 2 == 0 else kTo
                            w = []
                            if oi == 0:
                                w = [("AC", pb67_ac),
                                     ("DV", hilo_dv[1])]
                            E("tensor", lambda e, hh=hh, bank=bank,
                              kk=kk, ts=ts: e.matmul(
                                  PB[bank][:, (hh % 8) * 64:
                                           (hh % 8) * 64 + 64],
                                  kk[0:64,
                                     (hh // 2) * 512 + ts * 128:
                                     (hh // 2) * 512 +
                                     (ts + 1) * 128],
                                  ombd_sb[0:64, 0:64],
                                  start=(hh == 0 or hh == 8), stop=False,
                                  skip_group_check=True),
                              waits=w, inc="PE")
                        for bank in range(2):
                            E("tensor", lambda e, ts=ts, bank=bank:
                              e.matmul(
                                  PB[bank + 6][:, 0:512],
                                  nrTk[:, ts * 128:(ts + 1) * 128],
                                  shT_sb[:, bank * 512:(bank + 1) * 512],
                                  start=False, stop=True,
                                  skip_group_check=True),
                              inc="PE")
                        pe_pt = V("PE")
                        if DEBUG and s == 0 and qtr == 0 and ts == 0:
                            E("gpsimd", lambda e: e.dma_start(
                                dbg_nr.ap(), nrTk[:]),
                              waits=[("DV", hilo_dv[1])],
                              inc="SG", incn=16)
                        for bank in range(2):
                            E("scalar", lambda e, cl=cl, bank=bank:
                              e.activation(
                                  phikt[:, cl * 1024 + bank * 512:
                                        cl * 1024 + (bank + 1) * 512],
                                  PB[bank + 6][:, 0:512], AF.Exp,
                                  bias=negl8_sb[:, 0:1]),
                              waits=[("PE", pe_pt)], inc="AC")
                        pb67_ac = V("AC")
                ac_phi = V("AC")
                if STAGE < 6:
                    continue
                if DEBUG:
                    E("gpsimd", lambda e, s=s, qtr=qtr: e.dma_start(
                        dbg_pt.ap()[s * 4 + qtr], phikt[:]),
                      waits=[("AC", ac_phi)], inc="SG", incn=16)

                pb01s = V("DV")
                pb23f = [("DV", V("DV")), ("DV", V("DV"))]
                pb45f = V("DV")
                pb6f = V("DV")
                pb7f = V("DV")
                for cl in range(4):
                    c = qtr * 4 + cl
                    slotc = cl % 2
                    for w2_ in range(2):
                        for hh in range(8):
                            h = w2_ * 8 + hh
                            pk_ = phikT if h % 2 == 0 else phikTo
                            pq_ = phiqT if h % 2 == 0 else phiqTo
                            pcol = h // 2
                            w = []
                            if hh == 0:
                                w = [("AC", ac_phi), ("DV", pb01s)]
                            E("tensor", lambda e, pk_=pk_, pq_=pq_,
                              pcol=pcol, hh=hh, cl=cl: e.matmul(
                                  PB[hh % 2][:, (hh // 2) * 128:
                                             (hh // 2) * 128 + 128],
                                  pk_[0:64,
                                      pcol * 512 + cl * 128:
                                      pcol * 512 + (cl + 1) * 128],
                                  pq_[0:64,
                                      pcol * 512 + cl * 128:
                                      pcol * 512 + (cl + 1) * 128],
                                  start=(hh < 2), stop=(hh >= 6),
                                  skip_group_check=True),
                              waits=w, inc="PE")
                        pe_w = V("PE")
                        for par in range(2):
                            d0 = (w2_ * 8 + par * 4) * 128
                            E("vector", lambda e, par=par, d0=d0:
                              e.tensor_tensor(
                                  BT[:, d0:d0 + 512], PB[par][:, 0:512],
                                  mask_sb[:], ALU.mult),
                              waits=[("PE", pe_w)], inc="DV")
                        pb01s = V("DV")
                    dv_mask = V("DV")
                    for p in range(8):
                        nb = 2 + p % 2
                        fw = [("DV", dv_mask), ("DV", ssb_ev),
                              pb23f[p % 2]]
                        if c > 0:
                            E("tensor", lambda e, p=p, nb=nb, cl=cl:
                              e.matmul(
                                  PB[nb][:, 0:65],
                                  phiqT[0:64, p * 512 + cl * 128:
                                        p * 512 + (cl + 1) * 128],
                                  Ssb[:, (2 * p) * 65:(2 * p + 1) * 65],
                                  start=True, stop=False,
                                  skip_group_check=True),
                              waits=fw)
                            fw = []
                        E("tensor", lambda e, p=p, nb=nb, cl=cl, c=c:
                          e.matmul(
                              PB[nb][:, 0:65],
                              BT[:, bcol(2 * p) * 128:
                                 (bcol(2 * p) + 1) * 128],
                              vp[:, cl * 1040 + (2 * p) * 65:
                                 cl * 1040 + (2 * p) * 65 + 65],
                              start=(c == 0), stop=False,
                              skip_group_check=True),
                          waits=fw, inc="PE")
                        if c > 0:
                            E("tensor", lambda e, p=p, nb=nb, cl=cl:
                              e.matmul(
                                  PB[nb][:, 65:130],
                                  phiqTo[0:64, p * 512 + cl * 128:
                                         p * 512 + (cl + 1) * 128],
                                  Ssb[:, (2 * p + 1) * 65:
                                      (2 * p + 2) * 65],
                                  start=False, stop=False,
                                  skip_group_check=True))
                        E("tensor", lambda e, p=p, nb=nb, cl=cl, c=c:
                          e.matmul(
                              PB[nb][:, 65:130],
                              BT[:, bcol(2 * p + 1) * 128:
                                 (bcol(2 * p + 1) + 1) * 128],
                              vp[:, cl * 1040 + (2 * p + 1) * 65:
                                 cl * 1040 + (2 * p + 1) * 65 + 65],
                              start=False, stop=True,
                              skip_group_check=True),
                          inc="PE")
                        pe_num = V("PE")
                        zq = (p % 2) * 4
                        E("vector", lambda e, nb=nb, zq=zq:
                          e.tensor_scalar(
                            zr[:, zq:zq + 2], PB[nb][:, 64:130:65], 1e-6,
                            None, ALU.add),
                          waits=[("PE", pe_num), ("AC", zr_free[p % 2])],
                          inc="DV")
                        E("vector", lambda e, zq=zq: e.reciprocal(
                            zr[:, zq + 2:zq + 4], zr[:, zq:zq + 2]),
                          waits=[("DV", V("DV"))], inc="DV")
                        dv_z = V("DV")
                        w_at = [("DV", dv_z)]
                        if p == 0:
                            w_at.append(("PE", attn_free))
                        E("scalar", lambda e, nb=nb, p=p, zq=zq:
                          e.activation(
                              attn[:, p * 128:p * 128 + 64],
                              PB[nb][:, 0:64], AF.Copy,
                              scale=zr[:, zq + 2:zq + 3]),
                          waits=w_at, inc="AC")
                        E("scalar", lambda e, nb=nb, p=p, zq=zq:
                          e.activation(
                              attn[:, p * 128 + 64:(p + 1) * 128],
                              PB[nb][:, 65:129], AF.Copy,
                              scale=zr[:, zq + 3:zq + 4]),
                          inc="AC")
                        pb23f[p % 2] = ("AC", V("AC"))
                        zr_free[p % 2] = V("AC")
                    dv_attn = V("AC")
                    for h in range(16):
                        sb_ = 4 + h // 8
                        w = []
                        if h == 0:
                            w = [("DV", ssb_ev), ("AC", ac_phi),
                                 ("DV", pb45f)]
                        E("tensor", lambda e, h=h, sb_=sb_, cl=cl:
                          e.matmul(
                              PB[sb_][0:64,
                                      (h % 8) * 64:(h % 8) * 64 + 64],
                              phikt[:, cl * 1024 + h * 64:
                                    cl * 1024 + (h + 1) * 64],
                              vp[:, cl * 1040 + h * 65:
                                 cl * 1040 + h * 65 + 64],
                              start=(cl == 0 and h % 8 == 0),
                              stop=(cl == 3), skip_group_check=True),
                          waits=w, inc="PE")
                    for h in range(16):
                        w = []
                        if h == 0 and cl == 0:
                            w = [("DV", pb7f)]
                        E("tensor", lambda e, h=h, cl=cl: e.matmul(
                            PB[7][0:64, h:h + 1],
                            phikt[:, cl * 1024 + h * 64:
                                  cl * 1024 + (h + 1) * 64],
                            vp[:, cl * 1040 + h * 65 + 64:
                               cl * 1040 + h * 65 + 65],
                            start=(cl == 0 and h == 0), stop=(cl == 3),
                            skip_group_check=True),
                          waits=w, inc="PE")
                    pe_su = V("PE")
                    qp = qtr % 2
                    for bank in range(2):
                        dst = Ssb.ap().rearrange(
                            "p (h j) -> p h j", h=16)[
                            :, bank * 8:(bank + 1) * 8, 0:64]
                        E("vector", lambda e, bank=bank, qp=qp, dst=dst:
                          e.tensor_tensor(
                              dst, PB[4 + bank][0:64, 0:512],
                              kvb.ap().rearrange(
                                  "p (b n) -> p b n", b=2)[
                                  :, qp, bank * 512:(bank + 1) * 512],
                              ALU.add),
                          waits=[("PE", pe_su)], inc="DV")
                    dstd = Ssb.ap().rearrange("p (h j) -> p h j", h=16)[
                        :, :, 64:65]
                    E("vector", lambda e, qp=qp, dstd=dstd:
                      e.tensor_tensor(
                          dstd,
                          PB[7].ap()[0:64, 0:16].rearrange(
                              "p (h o) -> p h o", h=16),
                          dnb.ap()[0:64, qp * 16:(qp + 1) * 16].rearrange(
                              "p (h o) -> p h o", h=16),
                          ALU.add),
                      waits=[("PE", pe_su)], inc="DV")
                    if cl == 3:
                        E("vector", lambda e, qp=qp: e.tensor_tensor(
                            kvb[0:64, (1 - qp) * 1024:
                                (1 - qp) * 1024 + 512],
                            PB[4][0:64, 0:512],
                            kvb[0:64, qp * 1024:qp * 1024 + 512],
                            ALU.add),
                          waits=[("DV", V("DV"))], inc="DV")
                        E("vector", lambda e, qp=qp: e.tensor_tensor(
                            kvb[0:64, (1 - qp) * 1024 + 512:
                                (1 - qp) * 1024 + 1024],
                            PB[5][0:64, 0:512],
                            kvb[0:64, qp * 1024 + 512:(qp + 1) * 1024],
                            ALU.add),
                          inc="DV")
                        E("vector", lambda e, qp=qp: e.tensor_tensor(
                            dnb[0:64, (1 - qp) * 16:(2 - qp) * 16],
                            PB[7][0:64, 0:16],
                            dnb[0:64, qp * 16:(qp + 1) * 16],
                            ALU.add),
                          inc="DV")
                    ssb_ev = V("DV")
                    pb45f = ssb_ev
                    if DEBUG:
                        E("gpsimd", lambda e, s=s, qtr=qtr, cl=cl:
                          e.dma_start(
                              dbg_ss.ap()[(s * 4 + qtr) * 4 + cl],
                              Ssb[:]),
                          waits=[("DV", ssb_ev)], inc="SG", incn=16)
                    gi = cl % 4
                    for w2_ in range(2):
                        for j in range(4):
                            w = ([("AC", dv_attn), ("DV", pb6f)]
                                 if j == 0 else [])
                            E("tensor", lambda e, w2_=w2_,
                              j=j: e.transpose(
                                  PB[6][:, j * 128:(j + 1) * 128],
                                  attn[:, (w2_ * 4 + j) * 128:
                                       (w2_ * 4 + j) * 128 + 128],
                                  ident_sb[:]),
                              waits=w, inc="PE")
                        pe_tr = V("PE")
                        attn_free = pe_tr
                        dst = attnT.ap().rearrange(
                            "p (gi ic c) -> p gi ic c", gi=4, ic=8)[
                            :, gi, w2_ * 4:(w2_ + 1) * 4, :]
                        src = PB[6].ap().rearrange(
                            "p (ic c) -> p ic c", ic=4)
                        E("vector", lambda e, dst=dst, src=src:
                          e.tensor_copy(dst, src),
                          waits=[("PE", pe_tr)], inc="DV")
                        pb6f = V("DV")
                    attnT_rdy = V("DV")
                    if STAGE >= 7 and cl % 4 == 3:
                        for gi2 in range(4):
                            tg = qtr * 512 + (cl - 3 + gi2) * 128
                            xs = gi2 % 2
                            E("gpsimd", lambda e, s=s, tg=tg, xs=xs:
                              e.dma_start(
                                  x2[:, xs * 1024:(xs + 1) * 1024],
                                  x_d.ap()[s, tg:tg + 128, :]),
                              waits=[("DV", x2f[xs])], inc="SG", incn=16)
                            E("gpsimd", lambda e, s=s, tg=tg, xs=xs:
                              e.dma_start(
                                  g2[:, xs * 1024:(xs + 1) * 1024],
                                  g_s.ap()[s, tg:tg + 128, :]),
                              waits=[("DV", g2f[xs])], inc="SG", incn=16)
                            sx_c = V("SG")
                            os_ = 0
                            for eh in range(2):
                                for ic in range(8):
                                    w = []
                                    if ic == 0:
                                        w = [("DV", attnT_rdy),
                                             ("DV", pb7f)]
                                    E("tensor", lambda e, gi2=gi2, ic=ic,
                                      eh=eh: e.matmul(
                                          PB[7][:, 0:512],
                                          attnT[:, gi2 * 1024 +
                                                ic * 128:
                                                gi2 * 1024 +
                                                (ic + 1) * 128],
                                          wbf["wo"][:, ic * 1024 +
                                                    eh * 512:
                                                    ic * 1024 +
                                                    (eh + 1) * 512],
                                          start=(ic == 0), stop=False,
                                          skip_group_check=True),
                                      waits=w)
                                E("tensor", lambda e, eh=eh: e.matmul(
                                    PB[7][:, 0:512], ones2_sb[:],
                                    bohl_sb[:, eh * 512:(eh + 1) * 512],
                                    start=False, stop=True,
                                    skip_group_check=True),
                                  inc="PE")
                                pe_wo = V("PE")
                                E("vector", lambda e, xs=xs, eh=eh:
                                  e.tensor_tensor(
                                      sg[:], PB[7][:, 0:512],
                                      g2[:, xs * 1024 + eh * 512:
                                         xs * 1024 + (eh + 1) * 512],
                                      ALU.mult),
                                  waits=[("PE", pe_wo), ("SG", sx_c)],
                                  inc="DV")
                                E("vector", lambda e, xs=xs, eh=eh,
                                  os_=os_: e.tensor_tensor(
                                      outb[:, os_ * 1024 + eh * 512:
                                           os_ * 1024 +
                                           (eh + 1) * 512],
                                      sg[:],
                                      x2[:, xs * 1024 + eh * 512:
                                         xs * 1024 + (eh + 1) * 512],
                                      ALU.add),
                                  waits=[("DV", V("DV")),
                                         ("SC", outb_st[os_])],
                                  inc="DV")
                                pb7f = V("DV")
                            dv_out = V("DV")
                            E("scalar", lambda e, s=s, tg=tg, os_=os_:
                              e.dma_start(
                                  y_d.ap()[s, tg:tg + 128, :],
                                  outb[:, os_ * 1024:(os_ + 1) * 1024]),
                              waits=[("DV", dv_out)], inc="SC", incn=16)
                            outb_st[os_] = V("SC")
                            x2f[xs] = dv_out
                            g2f[xs] = dv_out

        if DEBUG:
            prog.barrier()
            E("sync", lambda e: e.dma_start(dbg_xnt.ap(), xnT_s.ap()),
              inc="SX", incn=16)
            E("sync", lambda e: e.dma_start(dbg_g.ap(), g_s.ap()),
              inc="SX", incn=16)
        prog.barrier()
        prog.build()

    return nc


def _host_consts():
    ident = np.eye(128, dtype=np.float32)
    mask = np.ascontiguousarray(
        np.tile(np.triu(np.ones((128, 128), dtype=np.float32)), (1, 4)))
    seln = np.zeros((128, 128), dtype=np.float32)
    for p in range(8):
        seln[0:64, p * 16 + 2 * p] = -0.5
        seln[64:128, p * 16 + 2 * p + 1] = -0.5
    shF = np.zeros((32, 1024), dtype=np.float32)
    for j in range(32):
        hsel = j % 16
        p = hsel // 2
        if hsel % 2 == 0:
            shF[j, p * 128:p * 128 + 64] = 1.0
        else:
            shF[j, p * 128 + 64:(p + 1) * 128] = 1.0
    shT = np.zeros((32, 1024), dtype=np.float32)
    for j in range(32):
        shT[j, (j % 16) * 64:((j % 16) + 1) * 64] = 1.0
    shFo = np.zeros((128, 512), dtype=np.float32)
    for j in range(32):
        hsel = j % 16
        if hsel % 2 == 1:
            p = hsel // 2
            shFo[j, p * 64:(p + 1) * 64] = 1.0
    hsel = np.zeros((128, 128), dtype=np.float32)
    hsel[0:64, 0:64] = -0.5
    hsel[64:128, 64:128] = -0.5
    return ident, mask, seln, shF, shT, shFo, hsel


def kernel(L0_attn_x, L0_attn_omega, L0_attn_wq, L0_attn_wk, L0_attn_wv,
           L0_attn_wo, L0_attn_bo, L0_ffn_W1, L0_ffn_b1, L0_ffn_W2,
           L0_ffn_b2):
    x = np.asarray(L0_attn_x, dtype=np.float32)
    ident, mask, seln, shF, shT, shFo, hsel = _host_consts()
    omT = np.ascontiguousarray(np.asarray(L0_attn_omega, np.float32).T)
    ombd = np.zeros((128, 128), dtype=np.float32)
    ombd[0:64, 0:64] = omT
    ombd[64:128, 64:128] = omT
    common = dict(
        wqT=np.ascontiguousarray(np.asarray(L0_attn_wq, np.float32).T),
        wkT=np.ascontiguousarray(np.asarray(L0_attn_wk, np.float32).T),
        wvT=np.ascontiguousarray(np.asarray(L0_attn_wv, np.float32).T),
        woT=np.ascontiguousarray(np.asarray(L0_attn_wo, np.float32).T),
        ombd=ombd,
        w1T=np.ascontiguousarray(np.asarray(L0_ffn_W1, np.float32).T),
        w2T=np.ascontiguousarray(np.asarray(L0_ffn_W2, np.float32).T),
        b1c=np.ascontiguousarray(
            np.asarray(L0_ffn_b1, np.float32).reshape(32, 128).T),
        b2c=np.ascontiguousarray(
            np.asarray(L0_ffn_b2, np.float32).reshape(8, 128).T),
        bo_row=np.asarray(L0_attn_bo, np.float32).reshape(1, D),
        ident=ident, mask4=mask, seln=seln, selhlF=shF, selhlT=shT,
        halfsel=hsel,
    )
    nc = build_nc()
    in_maps = []
    for c in range(NCORES):
        m = dict(common)
        m["x"] = np.ascontiguousarray(x[2 * c:2 * c + 2])
        in_maps.append(m)
    res = run_bass_kernel_spmd(nc, in_maps, core_ids=list(range(NCORES)))
    out = np.concatenate([r["y"].reshape(SEQ, N, D) for r in res.results],
                         axis=0)
    return np.ascontiguousarray(out, dtype=np.float32)


# version 17 (final)
# speedup vs baseline: 1.0417x; 1.0417x over previous
import contextlib
import math
import numpy as np
import concourse.bass as bass
import concourse.mybir as mybir
from concourse.bass_utils import run_bass_kernel_spmd

F32 = mybir.dt.float32
BF16 = mybir.dt.bfloat16
AF = mybir.ActivationFunctionType
ALU = mybir.AluOpType
AX = mybir.AxisListType

NCORES = 8
SEQ = 2
N = 2048
D = 1024
FF = 4096
LN8 = math.log(8.0)
TT1 = 256
NT1 = N // TT1
ENGINES = ("sync", "gpsimd", "scalar", "vector", "tensor")


class Prog:

    def __init__(self, nc):
        self.nc = nc
        self.items = {e: [] for e in ENGINES}
        self.sems = {}
        self.pending = {e: [] for e in ENGINES}

    def sem(self, name):
        self.sems[name] = [self.nc.alloc_semaphore(name), 0]

    def val(self, name):
        return self.sems[name][1]

    def emit(self, eng, fn, waits=(), inc=None, incn=1):
        wl = []
        for (s, v) in waits:
            assert v <= self.sems[s][1], (s, v, self.sems[s][1])
            if v > 0:
                wl.append((self.sems[s][0], v))
        if self.pending[eng]:
            wl = self.pending[eng] + wl
            self.pending[eng] = []
        handle = None
        if inc is not None:
            self.sems[inc][1] += incn
            handle = self.sems[inc][0]
        self.items[eng].append((fn, wl, handle, incn))

    def barrier(self):
        snap = [(self.sems[s][0], self.sems[s][1]) for s in self.sems
                if self.sems[s][1] > 0]
        for e in ENGINES:
            self.pending[e] = self.pending[e] + list(snap)

    def build(self):
        with self.nc.Block() as blk:
            for engname in ENGINES:
                items = self.items[engname]
                tail = self.pending[engname]
                if not items and not tail:
                    continue

                def body(eng, items=items, tail=tail):
                    for fn, wl, handle, incn in items:
                        for h, v in wl:
                            eng.wait_ge(h, v)
                        instr = fn(eng)
                        if handle is not None:
                            instr.then_inc(handle, incn)
                    for h, v in tail:
                        eng.wait_ge(h, v)

                getattr(blk, engname)(body)
        self.items = {e: [] for e in ENGINES}
        self.pending = {e: [] for e in ENGINES}


def bcol(h):
    return (h // 8) * 8 + (h % 2) * 4 + (h % 8) // 2


DEBUG = False
STAGE = 99


def build_nc():
    nc = bass.Bass("TRN2", target_bir_lowering=False,
                   detect_race_conditions=False)

    x_d = nc.dram_tensor("x", [SEQ, N, D], F32, kind="ExternalInput")
    wT_d = {k: nc.dram_tensor(k, [D, D], F32, kind="ExternalInput")
            for k in ("wqT", "wkT", "wvT", "woT")}
    ombd_d = nc.dram_tensor("ombd", [128, 128], F32, kind="ExternalInput")
    w1T_d = nc.dram_tensor("w1T", [D, FF], F32, kind="ExternalInput")
    w2T_d = nc.dram_tensor("w2T", [FF, D], F32, kind="ExternalInput")
    b1c_d = nc.dram_tensor("b1c", [128, 32], F32, kind="ExternalInput")
    b2c_d = nc.dram_tensor("b2c", [128, 8], F32, kind="ExternalInput")
    bo_d = nc.dram_tensor("bo_row", [1, D], F32, kind="ExternalInput")
    ident_d = nc.dram_tensor("ident", [128, 128], F32, kind="ExternalInput")
    mask_d = nc.dram_tensor("mask4", [128, 512], F32, kind="ExternalInput")
    seln_d = nc.dram_tensor("seln", [128, 128], F32, kind="ExternalInput")
    hsel_d = nc.dram_tensor("halfsel", [128, 128], F32,
                            kind="ExternalInput")
    shF_d = nc.dram_tensor("selhlF", [32, 1024], F32, kind="ExternalInput")
    shT_d = nc.dram_tensor("selhlT", [32, 1024], F32, kind="ExternalInput")
    xnT_s = nc.dram_tensor("xnT_s", [SEQ, 8, 128, N], BF16, kind="Internal")
    g_s = nc.dram_tensor("g_s", [SEQ, N, D], F32, kind="Internal")
    bohl_scr = nc.dram_tensor("bohl_scr", [2, 1024], BF16, kind="Internal")
    y_d = nc.dram_tensor("y", [SEQ, N, D], F32, kind="ExternalOutput")
    if DEBUG:
        dbg_xnt = nc.dram_tensor("dbg_xnt", [SEQ, 8, 128, N], BF16,
                                 kind="ExternalOutput")
        dbg_g = nc.dram_tensor("dbg_g", [SEQ, N, D], F32,
                               kind="ExternalOutput")
        dbg_pt = nc.dram_tensor("dbg_pt", [8, 128, 4096], F32,
                                kind="ExternalOutput")
        dbg_w1 = nc.dram_tensor("dbg_w1", [128, 8 * FF], BF16,
                                kind="ExternalOutput")
        dbg_w2 = nc.dram_tensor("dbg_w2", [128, 32 * D], BF16,
                                kind="ExternalOutput")
        dbg_h = nc.dram_tensor("dbg_h", [128, 2 * 256], BF16,
                               kind="ExternalOutput")
        dbg_nr = nc.dram_tensor("dbg_nr", [32, 512], F32,
                                kind="ExternalOutput")
        dbg_ss = nc.dram_tensor("dbg_ss", [32, 64, 1040], F32,
                                kind="ExternalOutput")

    A = nc.alloc_sbuf_tensor
    ident_sb = A("ident_sb", [128, 128], F32)
    mask_sb = A("mask_sb", [128, 512], BF16)
    seln_sb = A("seln_sb", [128, 128], F32)
    hsel_sb = A("hsel_sb", [128, 128], F32)
    shF_sb = A("shF_sb", [32, 1024], BF16)
    shT_sb = A("shT_sb", [32, 1024], BF16)
    ombd_sb = A("ombd_sb", [128, 128], BF16)
    b1c_sb = A("b1c_sb", [128, 32], F32)
    b2c_sb = A("b2c_sb", [128, 8], F32)
    bohl_sb = A("bohl_sb", [2, D], BF16)
    ones2_sb = A("ones2_sb", [2, 128], BF16)
    onesc_sb = A("onesc_sb", [128, 1], F32)
    onesr_sb = A("onesr_sb", [1, 128], F32)
    negl8_sb = A("negl8_sb", [128, 1], F32)
    eps_sb = A("eps_sb", [128, 1], F32)

    PB = [nc.alloc_psum_tensor(f"PB{i}", [128, 512], F32) for i in range(8)]

    prog = Prog(nc)
    for s in ("SX", "SY", "SC", "SG", "PE", "DV", "AC"):
        prog.sem(s)
    E = prog.emit
    V = prog.val

    for dst, src in ((ident_sb, ident_d), (seln_sb, seln_d),
                     (b1c_sb, b1c_d), (b2c_sb, b2c_d),
                     (hsel_sb, hsel_d)):
        E("sync", lambda e, d=dst, s=src: e.dma_start(d[:], s.ap()),
          inc="SX", incn=16)
    sx_setup = V("SX")
    for dst, src in ((ombd_sb, ombd_d), (shF_sb, shF_d), (shT_sb, shT_d),
                     (mask_sb, mask_d)):
        E("gpsimd", lambda e, d=dst, s=src: e.dma_start(d[:], s.ap()),
          inc="SG", incn=16)
    sg_setup = V("SG")
    E("vector", lambda e: e.memset(ones2_sb[:], 1.0), inc="DV")
    E("vector", lambda e: e.memset(onesc_sb[:], 1.0), inc="DV")
    E("vector", lambda e: e.memset(onesr_sb[:], 1.0), inc="DV")
    E("vector", lambda e: e.memset(negl8_sb[:], -LN8), inc="DV")
    E("vector", lambda e: e.memset(eps_sb[:], 1e-5), inc="DV")
    dv_ones = V("DV")

    with contextlib.ExitStack() as st1:
        S1 = lambda name, shape, dt: st1.enter_context(
            nc.sbuf_tensor(name, shape, dt))
        bor_sb = S1("bor_sb", [128, 8], F32)
        bohi_b = S1("bohi_b", [128, 8], BF16)
        bohi_f = S1("bohi_f", [128, 8], F32)
        bolo_f = S1("bolo_f", [128, 8], F32)
        bolo_b = S1("bolo_b", [128, 8], BF16)
        w1bf = S1("w1bf", [128, 8 * FF], BF16)
        w2bf = S1("w2bf", [128, 32 * D], BF16)
        wst = S1("wst", [128, 2 * 1024], F32)
        wst2 = S1("wst2", [128, 2 * 1024], F32)
        tn = S1("tn", [128, 1024], BF16)
        tn2 = S1("tn2", [128, 1024], BF16)
        acc = S1("acc", [128, 4], F32)
        acc2 = S1("acc2", [128, 4], F32)
        inv1 = S1("inv1", [128, 1], F32)
        inv2 = S1("inv2", [128, 1], F32)
        sc1 = S1("sc1", [1, 4], F32)
        x1 = S1("x1", [128, 3 * 1024], F32)
        scr = S1("scr", [128, 2 * 1024], F32)
        lnc = S1("lnc", [128, 16], F32)
        xnT1 = S1("xnT1", [128, 2 * 2048], BF16)
        hbf = S1("hbf", [128, 2 * 256], BF16)
        gTr = S1("gTr", [128, 2 * 256], F32)
        gtok = S1("gtok", [128, 2048], F32)

        E("sync", lambda e: e.dma_start(
            bor_sb[:], bo_d.ap().rearrange("o (p c) -> (o p) c", p=128)),
          inc="SX", incn=16)
        sx_bo = V("SX")
        E("vector", lambda e: e.tensor_copy(bohi_b[:], bor_sb[:]),
          waits=[("SX", sx_bo), ("DV", dv_ones)], inc="DV")
        E("vector", lambda e: e.tensor_copy(bohi_f[:], bohi_b[:]),
          waits=[("DV", V("DV"))], inc="DV")
        E("vector", lambda e: e.tensor_tensor(bolo_f[:], bor_sb[:],
                                              bohi_f[:], ALU.subtract),
          waits=[("DV", V("DV"))], inc="DV")
        E("vector", lambda e: e.tensor_copy(bolo_b[:], bolo_f[:]),
          waits=[("DV", V("DV"))], inc="DV")
        dv_bolo = V("DV")
        E("sync", lambda e: e.dma_start(
            bohl_scr.ap()[0:1, :].rearrange("o (p c) -> (o p) c", p=128),
            bohi_b[:]),
          waits=[("DV", dv_bolo)], inc="SX", incn=16)
        E("sync", lambda e: e.dma_start(
            bohl_scr.ap()[1:2, :].rearrange("o (p c) -> (o p) c", p=128),
            bolo_b[:]),
          waits=[("DV", dv_bolo)], inc="SX", incn=16)
        sx_bscr = V("SX")
        E("sync", lambda e: e.dma_start(bohl_sb[:], bohl_scr.ap()),
          waits=[("SX", sx_bscr)], inc="SX", incn=16)

        mats = [
            dict(mat=w1T_d, nco=FF, dst=w1bf, inv=inv1, stg=wst,
                 accb=acc, tnb=tn, bank=0, scc=0, per=4, dq="SX"),
            dict(mat=w2T_d, nco=D, dst=w2bf, inv=inv2, stg=wst2,
                 accb=acc2, tnb=tn2, bank=1, scc=2, per=1, dq="SY"),
        ]
        for m in mats:
            E("vector", lambda e, m=m: e.memset(m["accb"][:, 1:2], 0.0),
              inc="DV")
        dv_ms = V("DV")
        tguard = [[dv_ms, dv_ms], [dv_ms, dv_ms]]
        for k in range(32):
            for i, m in enumerate(mats):
                rk, ck = k // m["per"], k % m["per"]
                slot = k % 2
                src = m["mat"].ap()[rk * 128:(rk + 1) * 128,
                                    ck * 1024:(ck + 1) * 1024]
                E("sync", lambda e, s=src, m=m, sl=slot: e.dma_start(
                    m["stg"][:, sl * 1024:(sl + 1) * 1024], s),
                  waits=[("DV", tguard[i][slot])], inc=m["dq"], incn=16)
                sxv = V(m["dq"])
                E("vector", lambda e, m=m, sl=slot: e.reduce_sum(
                    m["accb"][:, 2:3],
                    m["stg"][:, sl * 1024:(sl + 1) * 1024],
                    axis=AX.X, apply_absolute_value=True),
                  waits=[(m["dq"], sxv)], inc="DV")
                dv_red = V("DV")
                E("vector", lambda e, m=m, k=k: e.tensor_add(
                    m["accb"][:, k % 2:k % 2 + 1],
                    m["accb"][:, (k + 1) % 2:(k + 1) % 2 + 1],
                    m["accb"][:, 2:3]),
                  waits=[("DV", dv_red)], inc="DV")
                tguard[i][slot] = V("DV")
        dv_invc = [0, 0]
        for i, m in enumerate(mats):
            c_acc = tguard[i][1 - 32 % 2]
            E("tensor", lambda e, m=m: e.matmul(
                PB[m["bank"]][0:1, 0:1], m["accb"][:, 1:2], onesc_sb[:],
                start=True, stop=True, skip_group_check=True),
              waits=[("DV", c_acc)], inc="PE")
            pe_tot = V("PE")
            E("vector", lambda e, m=m: e.tensor_scalar(
                sc1[:, m["scc"]:m["scc"] + 1], PB[m["bank"]][0:1, 0:1],
                1.0 / (D * FF), 1e-8, ALU.mult, ALU.max),
              waits=[("PE", pe_tot)], inc="DV")
            E("vector", lambda e, m=m: e.reciprocal(
                sc1[:, m["scc"] + 1:m["scc"] + 2],
                sc1[:, m["scc"]:m["scc"] + 1]),
              waits=[("DV", V("DV"))], inc="DV")
            dv_inv = V("DV")
            E("tensor", lambda e, m=m: e.matmul(
                PB[m["bank"]][:, 0:1], onesr_sb[:],
                sc1[:, m["scc"] + 1:m["scc"] + 2],
                start=True, stop=True, skip_group_check=True),
              waits=[("DV", dv_inv)], inc="PE")
            pe_bc = V("PE")
            E("vector", lambda e, m=m: e.tensor_copy(
                m["inv"][:], PB[m["bank"]][:, 0:1]),
              waits=[("PE", pe_bc)], inc="DV")
            dv_invc[i] = V("DV")
        tguard = [[V("DV")] * 2, [V("DV")] * 2]
        tdone = [0, 0]
        for k in range(32):
            for i, m in enumerate(mats):
                rk, ck = k // m["per"], k % m["per"]
                slot = k % 2
                src = m["mat"].ap()[rk * 128:(rk + 1) * 128,
                                    ck * 1024:(ck + 1) * 1024]
                E("sync", lambda e, s=src, m=m, sl=slot: e.dma_start(
                    m["stg"][:, sl * 1024:(sl + 1) * 1024], s),
                  waits=[("DV", tguard[i][slot])], inc=m["dq"], incn=16)
                sxv = V(m["dq"])
                d0 = rk * m["nco"] + ck * 1024
                E("vector", lambda e, m=m, sl=slot, d0=d0: e.tensor_scalar(
                    m["dst"][:, d0:d0 + 1024],
                    m["stg"][:, sl * 1024:(sl + 1) * 1024],
                    m["inv"][:, 0:1], 0.5, ALU.mult, ALU.is_gt),
                  waits=[(m["dq"], sxv), ("DV", dv_invc[i])])
                E("vector", lambda e, m=m, sl=slot: e.tensor_scalar(
                    m["tnb"][:], m["stg"][:, sl * 1024:(sl + 1) * 1024],
                    m["inv"][:, 0:1], -0.5, ALU.mult, ALU.is_lt),
                  inc="DV")
                tguard[i][slot] = V("DV")
                E("vector", lambda e, m=m, d0=d0: e.tensor_tensor(
                    m["dst"][:, d0:d0 + 1024], m["dst"][:, d0:d0 + 1024],
                    m["tnb"][:], ALU.subtract),
                  waits=[("DV", V("DV"))], inc="DV")
                tdone[i] = V("DV")
        tern1_dv, tern2_sg = tdone[0], tdone[1]
        if DEBUG:
            E("sync", lambda e: e.dma_start(dbg_w1.ap(), w1bf[:]),
              waits=[("DV", tern1_dv)], inc="SX", incn=16)
            E("sync", lambda e: e.dma_start(dbg_w2.ap(), w2bf[:]),
              waits=[("DV", tern2_sg)], inc="SX", incn=16)

        xg = [0, 0, 0]
        scr_g = [0, 0]
        pb01f = [0, 0]
        out2_free = 0
        pb6_ev = 0
        pb7_ev = 0
        xnt_rd = [0, 0]
        xnt_use = [0, 0]
        gtok_st = 0

        def emit_ffn(s, t0, slot):
            nonlocal out2_free, gtok_st, pb7_ev
            pe_mm2_last = 0
            ac_hv = [0, 0]
            h_free = [0, 0]
            for F in range(33):
                if F < 32:
                    b = F % 2
                    for Dc in range(8):
                        w = []
                        if Dc == 0:
                            w = [("DV", tern1_dv), ("DV", xnt_rd[slot]),
                                 ("AC", pb01f[b])]
                            if F == 0:
                                w += [("AC", out2_free),
                                      ("DV", tern2_sg)]
                        E("tensor", lambda e, F=F, Dc=Dc, b=b, slot=slot:
                          e.matmul(PB[b][:, 0:256],
                                   w1bf[:, Dc * FF + F * 128:
                                        Dc * FF + (F + 1) * 128],
                                   xnT1[:, slot * 2048 + Dc * 256:
                                        slot * 2048 + (Dc + 1) * 256],
                                   start=(Dc == 0), stop=(Dc == 7),
                                   skip_group_check=True),
                          waits=w, inc=("PE" if Dc == 7 else None))
                    pe_mm1 = V("PE")
                if F >= 1:
                    Fm = F - 1
                    hs2 = Fm % 2
                    for Ec in range(8):
                        ob = 2 + Ec // 2
                        oc = (Ec % 2) * 256
                        w = [("AC", ac_hv[hs2])] if Ec == 0 else []
                        E("tensor", lambda e, Fm=Fm, Ec=Ec, ob=ob, oc=oc,
                          hs2=hs2: e.matmul(
                              PB[ob][:, oc:oc + 256],
                              w2bf[:, Fm * D + Ec * 128:
                                   Fm * D + (Ec + 1) * 128],
                              hbf[:, hs2 * 256:(hs2 + 1) * 256],
                              start=(Fm == 0 and Ec % 2 == 0),
                              stop=(Fm == 31), skip_group_check=True),
                          waits=w,
                          inc=("PE" if (Ec == 7 or Fm == 31) else None))
                    pe_mm2_last = V("PE")
                    h_free[hs2] = pe_mm2_last
                if F < 32:
                    hs = F % 2
                    E("scalar", lambda e, F=F, b=b, hs=hs: e.activation(
                        hbf[:, hs * 256:(hs + 1) * 256], PB[b][:, 0:256],
                        AF.Gelu, bias=b1c_sb[:, F:F + 1]),
                      waits=[("PE", pe_mm1), ("PE", h_free[hs])],
                      inc="AC")
                    pb01f[b] = V("AC")
                    ac_hv[hs] = V("AC")
            gtr_free = [0, 0]
            for Ec in range(8):
                ob = 2 + Ec // 2
                oc = (Ec % 2) * 256
                gs = Ec % 2
                E("scalar", lambda e, ob=ob, oc=oc, gs=gs, Ec=Ec:
                  e.activation(gTr[:, gs * 256:(gs + 1) * 256],
                               PB[ob][:, oc:oc + 256], AF.Silu,
                               bias=b2c_sb[:, Ec:Ec + 1]),
                  waits=[("PE", pe_mm2_last), ("PE", gtr_free[gs])],
                  inc="AC")
                ac_g = V("AC")
                for ts in range(2):
                    w = ([("AC", ac_g), ("DV", pb7_ev)] if ts == 0 else [])
                    E("tensor", lambda e, ts=ts, gs=gs: e.transpose(
                        PB[7][:, gs * 256 + ts * 128:
                              gs * 256 + (ts + 1) * 128],
                        gTr[:, gs * 256 + ts * 128:
                            gs * 256 + (ts + 1) * 128],
                        ident_sb[:]),
                      waits=w, inc="PE")
                gtr_free[gs] = V("PE")
                if Ec % 2 == 1:
                    pe_gt = V("PE")
                    e0 = Ec - 1
                    dst = gtok.ap().rearrange(
                        "p (ts ec c) -> p ec ts c", ts=2, ec=8)[
                        :, e0:e0 + 2, :, :]
                    src = PB[7].ap().rearrange(
                        "p (ec ts c) -> p ec ts c", ec=2, ts=2)
                    E("vector", lambda e, dst=dst, src=src:
                      e.tensor_copy(dst, src),
                      waits=[("PE", pe_gt), ("SC", gtok_st)], inc="DV")
                    pb7_ev = V("DV")
            if DEBUG and s == SEQ - 1 and t0 == N - TT1:
                E("sync", lambda e: e.dma_start(dbg_h.ap(), hbf[:]),
                  waits=[("PE", pe_mm2_last)], inc="SX", incn=16)
            out2_free = V("AC")
            dv_g = V("DV")
            E("scalar", lambda e, s=s, t0=t0: e.dma_start(
                g_s.ap()[s, t0:t0 + TT1, :].rearrange(
                    "(ts p) e -> p ts e", p=128),
                gtok.ap().rearrange("p (ts e) -> p ts e", ts=2)),
              waits=[("DV", dv_g)], inc="SC", incn=16)
            gtok_st = V("SC")
            xnt_use[slot] = pe_mm2_last

        for it in range(SEQ * NT1):
            s, T = it // NT1, it % NT1
            t0 = T * TT1
            slot = it % 2
            dv_xn = [0, 0]
            for ts in range(2):
                xs = (2 * it + ts) % 3
                xap = x_d.ap()[s, t0 + ts * 128:t0 + (ts + 1) * 128, :]
                E("sync", lambda e, xap=xap, xs=xs: e.dma_start(
                    x1[:, xs * 1024:(xs + 1) * 1024], xap),
                  waits=[("DV", xg[xs])], inc="SX", incn=16)
                sxv = V("SX")
                c = ts * 8
                E("scalar", lambda e, xs=xs, ts=ts, c=c: e.activation(
                    scr[:, ts * 1024:(ts + 1) * 1024],
                    x1[:, xs * 1024:(xs + 1) * 1024], AF.Square,
                    accum_out=lnc[:, c:c + 1]),
                  waits=[("SX", sxv), ("PE", scr_g[ts])], inc="AC")
                ac_sq = V("AC")
                E("vector", lambda e, xs=xs, c=c: e.reduce_sum(
                    lnc[:, c + 1:c + 2], x1[:, xs * 1024:(xs + 1) * 1024],
                    axis=AX.X),
                  waits=[("SX", sxv)], inc="DV")
                E("vector", lambda e, c=c: e.tensor_scalar(
                    lnc[:, c + 2:c + 3], lnc[:, c + 1:c + 2], 1.0 / D, None,
                    ALU.mult),
                  waits=[("DV", V("DV"))], inc="DV")
                E("vector", lambda e, c=c: e.tensor_tensor(
                    lnc[:, c + 3:c + 4], lnc[:, c + 2:c + 3],
                    lnc[:, c + 2:c + 3], ALU.mult),
                  waits=[("DV", V("DV"))], inc="DV")
                E("vector", lambda e, c=c: e.scalar_tensor_tensor(
                    lnc[:, c + 4:c + 5], lnc[:, c:c + 1], 1.0 / D,
                    lnc[:, c + 3:c + 4], ALU.mult, ALU.subtract),
                  waits=[("DV", V("DV")), ("AC", ac_sq)], inc="DV")
                dv_var = V("DV")
                E("scalar", lambda e, c=c: e.activation(
                    lnc[:, c + 5:c + 6], lnc[:, c + 4:c + 5], AF.Ln,
                    bias=eps_sb[:, 0:1]),
                  waits=[("DV", dv_var), ("DV", dv_ones)], inc="AC")
                E("scalar", lambda e, c=c: e.activation(
                    lnc[:, c + 6:c + 7], lnc[:, c + 5:c + 6], AF.Exp,
                    scale=-0.5),
                  waits=[("AC", V("AC"))], inc="AC")
                ac_rstd = V("AC")
                E("vector", lambda e, xs=xs, ts=ts, c=c: e.tensor_scalar(
                    scr[:, ts * 1024:(ts + 1) * 1024],
                    x1[:, xs * 1024:(xs + 1) * 1024],
                    lnc[:, c + 2:c + 3], lnc[:, c + 6:c + 7],
                    ALU.subtract, ALU.mult),
                  waits=[("AC", ac_rstd)], inc="DV")
                dv_xn[ts] = V("DV")
                xg[xs] = V("DV")
            for ts in range(2):
                for wave in range(2):
                    for j in range(4):
                        cidx = wave * 4 + j
                        w = ([("DV", dv_xn[ts]), ("DV", pb6_ev),
                              ("SX", sx_setup)] if j == 0 else [])
                        E("tensor", lambda e, ts=ts, cidx=cidx, j=j:
                          e.transpose(PB[6][:, j * 128:(j + 1) * 128],
                                      scr[:, ts * 1024 + cidx * 128:
                                          ts * 1024 + (cidx + 1) * 128],
                                      ident_sb[:]),
                          waits=w, inc="PE")
                    pe_tp = V("PE")
                    dst = xnT1.ap()[:, slot * 2048:(slot + 1) * 2048]
                    dst = dst.rearrange("p (dc t) -> p dc t", dc=8)[
                        :, wave * 4:(wave + 1) * 4,
                        ts * 128:(ts + 1) * 128]
                    src4 = PB[6].ap().rearrange("p (dc t) -> p dc t", dc=4)
                    E("vector", lambda e, dst=dst, src4=src4:
                      e.tensor_copy(dst, src4),
                      waits=[("PE", pe_tp), ("PE", xnt_use[slot])],
                      inc="DV")
                    pb6_ev = V("DV")
            scr_g[0] = scr_g[1] = V("PE")
            xnt_rd[slot] = V("DV")
            E("scalar", lambda e, s=s, t0=t0, slot=slot: e.dma_start(
                xnT_s.ap()[s].rearrange("c p t -> p c t")[
                    :, :, t0:t0 + TT1],
                xnT1.ap()[:, slot * 2048:(slot + 1) * 2048].rearrange(
                    "p (c t) -> p c t", c=8)),
              waits=[("DV", xnt_rd[slot])], inc="SC", incn=16)
            if it > 0:
                emit_ffn(*ffn_prev)
            ffn_prev = (s, t0, slot)

        emit_ffn(*ffn_prev)
        prog.build()

    prog.barrier()
    if STAGE <= 1:
        prog.build()
        return nc
    with contextlib.ExitStack() as st2:
        S2 = lambda name, shape, dt: st2.enter_context(
            nc.sbuf_tensor(name, shape, dt))
        wbf = {k: S2(k + "bf", [128, 8 * 1024], BF16)
               for k in ("wq", "wk", "wv", "wo")}
        xnT2 = S2("xnT2", [128, 8 * 512], BF16)
        qT = S2("qT", [128, 8 * 512], BF16)
        kT = S2("kT", [128, 8 * 512], BF16)
        kTo = S2("kTo", [64, 8 * 512], BF16)
        sqp = S2("sqp", [128, 2 * 512], F32)
        nhb = S2("nhb", [128, 16], BF16)
        nhl = S2("nhl", [128, 32], F32)
        nrTk = S2("nrTk", [32, 512], BF16)
        phiqT = S2("phiqT", [128, 8 * 512], BF16)
        phiqTo = S2("phiqTo", [64, 8 * 512], BF16)
        phikT = S2("phikT", [128, 8 * 512], BF16)
        phikTo = S2("phikTo", [64, 8 * 512], BF16)
        phikt = S2("phikt", [128, 4 * 1024], BF16)
        vp = S2("vp", [128, 4 * 1040], BF16)
        Ssb = S2("Ssb", [64, 16 * 65], BF16)
        kvb = S2("kvb", [64, 2 * 1024], F32)
        dnb = S2("dnb", [64, 2 * 16], F32)
        BT = S2("BT", [128, 16 * 128], BF16)
        zr = S2("zr", [128, 8], F32)
        attn = S2("attn", [128, 1024], F32)
        attnT = S2("attnT", [128, 4 * 1024], BF16)
        x2 = S2("x2", [128, 2 * 1024], F32)
        g2 = S2("g2", [128, 2 * 1024], F32)
        sg = S2("sg", [128, 512], F32)
        outb = S2("outb", [128, 1024], F32)

        for k, src in (("wq", wT_d["wqT"]), ("wk", wT_d["wkT"]),
                       ("wv", wT_d["wvT"]), ("wo", wT_d["woT"])):
            E("gpsimd", lambda e, k=k, src=src: e.dma_start(
                wbf[k].ap().rearrange("p (c o) -> p c o", c=8),
                src.ap().rearrange("(c p) o -> p c o", p=128)),
              inc="SG", incn=16)
        sg_w = V("SG")
        E("vector", lambda e: e.memset(vp[:], 1.0), inc="DV")
        dv_vp0 = V("DV")

        xnT2_g = 0
        pb01f = [0, 0]
        pb23f = [("DV", 0), ("DV", 0)]
        zr_free = [0, 0]
        attn_free = 0
        pb3f = 0
        pb6f = 0
        pb7f = 0
        pb45f = 0
        sq_g = [0, 0]
        nhl_free = 0
        x2f = [0, 0]
        g2f = [0, 0]
        outb_st = [0]
        ssb_ev = 0

        for s in range(SEQ):
            E("vector", lambda e: e.memset(kvb[0:64, 0:1024], 0.0), inc="DV")
            E("vector", lambda e: e.memset(dnb[0:64, 0:16], 0.0), inc="DV")
            for qtr in range(4):
                pb23f = [("DV", V("DV")), ("DV", V("DV"))]
                pb45f = V("DV")
                pb2f = V("DV")
                dvsnap = V("DV")
                for T in range(1):
                    tg0 = qtr * 512
                    E("sync", lambda e, s=s, tg0=tg0: e.dma_start(
                        xnT2.ap().rearrange("p (c t) -> p c t", c=8),
                        xnT_s.ap()[s].rearrange("c p t -> p c t")[
                            :, :, tg0:tg0 + 512]),
                      waits=[("PE", xnT2_g)], inc="SX", incn=16)
                    sx_x = V("SX")
                    hilo_dv = [0, 0]
                    dv_scan = V("DV")
                    phb_free = [0, 0]
                    for idx, wname in ((0, "wq"), (1, "wk")):
                        qk = qT if idx == 0 else kT
                        prev = None
                        for p in range(9):
                            if p < 8:
                                for Dc in range(8):
                                    w = []
                                    if Dc == 0:
                                        w = [("SX", sx_x), ("SG", sg_w),
                                             ("AC", pb01f[p % 2])]
                                        if p == 0 and idx == 0 and T == 0:
                                            w.append(("DV", dvsnap))
                                    E("tensor", lambda e, wname=wname, p=p,
                                      Dc=Dc: e.matmul(
                                          PB[p % 2][:, 0:512],
                                          wbf[wname][:,
                                                     Dc * 1024 + p * 128:
                                                     Dc * 1024 +
                                                     (p + 1) * 128],
                                          xnT2[:, Dc * 512:(Dc + 1) * 512],
                                          start=(Dc == 0), stop=(Dc == 7),
                                          skip_group_check=True),
                                      waits=w,
                                      inc=("PE" if Dc == 7 else None))
                                pe_q = V("PE")
                            if p >= 1:
                                pm = p - 1
                                sl = pm % 2
                                dv_sq, ac_qkm = prev
                                if idx == 0:
                                    w = [("SG", sg_setup), ("AC", ac_qkm),
                                         ("AC", phb_free[pm % 2])]
                                    if pm < 2:
                                        w.append(("DV", dv_scan))
                                    E("tensor", lambda e, pm=pm: e.matmul(
                                        PB[2 + pm % 2][:, 0:512],
                                        ombd_sb[:],
                                        qT[:, pm * 512:(pm + 1) * 512],
                                        start=True, stop=False,
                                        skip_group_check=True),
                                      waits=w)
                                    E("tensor", lambda e, pm=pm, sl=sl:
                                      e.matmul(
                                          PB[2 + pm % 2][:, 0:512],
                                          hsel_sb[:],
                                          sqp[:, sl * 512:(sl + 1) * 512],
                                          start=False, stop=True,
                                          skip_group_check=True),
                                      waits=[("DV", dv_sq),
                                             ("SX", sx_setup)],
                                      inc="PE")
                                    sq_g[sl] = V("PE")
                                    pe_phi = V("PE")
                                    E("scalar", lambda e, pm=pm:
                                      e.activation(
                                          phiqT[:, pm * 512:
                                                (pm + 1) * 512],
                                          PB[2 + pm % 2][:, 0:512],
                                          AF.Exp,
                                          bias=negl8_sb[:, 0:1]),
                                      waits=[("PE", pe_phi),
                                             ("DV", dv_ones)],
                                      inc="AC")
                                    E("scalar", lambda e, pm=pm:
                                      e.activation(
                                          phiqTo[:, pm * 512:
                                                 (pm + 1) * 512],
                                          PB[2 + pm % 2][64:128, 0:512],
                                          AF.Exp,
                                          bias=negl8_sb[0:64, 0:1]),
                                      inc="AC")
                                    phb_free[pm % 2] = V("AC")
                                else:
                                    for ts in range(4):
                                        w = [("DV", dv_sq)]
                                        if pm == 0 and ts == 0:
                                            w += [("DV", pb2f),
                                                  ("AC", phb_free[0])]
                                        E("tensor", lambda e, pm=pm, ts=ts,
                                          sl=sl: e.matmul(
                                              PB[2][:, 64 + ts * 16:
                                                    64 + (ts + 1) * 16],
                                              sqp[:, sl * 512 + ts * 128:
                                                  sl * 512 +
                                                  (ts + 1) * 128],
                                              seln_sb[:, pm * 16:
                                                      (pm + 1) * 16],
                                              start=(pm == 0 and ts == 0),
                                              stop=(pm == 7 and ts == 3),
                                              skip_group_check=True),
                                          waits=w, inc="PE")
                                    sq_g[sl] = V("PE")
                            if p < 8:
                                E("scalar", lambda e, qk=qk, p=p:
                                  e.activation(
                                      qk[:, p * 512:(p + 1) * 512],
                                      PB[p % 2][:, 0:512], AF.Copy),
                                  waits=[("PE", pe_q)], inc="AC")
                                ac_qk = V("AC")
                                if idx == 1:
                                    E("scalar", lambda e, p=p:
                                      e.activation(
                                          kTo[:, p * 512:(p + 1) * 512],
                                          PB[p % 2][64:128, 0:512],
                                          AF.Copy),
                                      inc="AC")
                                pb01f[p % 2] = V("AC")
                                E("vector", lambda e, qk=qk, p=p:
                                  e.tensor_tensor(
                                      sqp[:, (p % 2) * 512:
                                          (p % 2 + 1) * 512],
                                      qk[:, p * 512:(p + 1) * 512],
                                      qk[:, p * 512:(p + 1) * 512],
                                      ALU.mult),
                                  waits=[("AC", ac_qk),
                                         ("PE", sq_g[p % 2])],
                                  inc="DV")
                                prev = (V("DV"), ac_qk)
                        if idx == 0:
                            continue
                        pe_norm = V("PE")
                        for ts in range(4):
                            off = 64 + ts * 16
                            E("vector", lambda e, off=off: e.tensor_copy(
                                nhb[:], PB[2][:, off:off + 16]),
                              waits=[("PE", pe_norm), ("DV", V("DV")),
                                     ("PE", nhl_free)],
                              inc="DV")
                            E("vector", lambda e: e.tensor_copy(
                                nhl[:, 0:16], nhb[:]),
                              waits=[("DV", V("DV"))], inc="DV")
                            E("vector", lambda e, off=off: e.tensor_tensor(
                                nhl[:, 16:32], PB[2][:, off:off + 16],
                                nhl[:, 0:16], ALU.subtract),
                              waits=[("DV", V("DV"))], inc="DV")
                            dv_nhl = V("DV")
                            E("tensor", lambda e, ts=ts: e.transpose(
                                PB[3][0:32, ts * 128:(ts + 1) * 128],
                                nhl[:], ident_sb[:]),
                              waits=[("DV", dv_nhl), ("DV", pb3f)],
                              inc="PE")
                            pe_ntp = V("PE")
                            nhl_free = pe_ntp
                            E("vector", lambda e, ts=ts:
                              e.tensor_copy(
                                  nrTk[0:32, ts * 128:(ts + 1) * 128],
                                  PB[3][0:32, ts * 128:(ts + 1) * 128]),
                              waits=[("PE", pe_ntp)], inc="DV")
                            pb3f = V("DV")
                        hilo_dv[1] = V("DV")
                        pb2f = V("DV")
                    if STAGE < 3:
                        continue
                    for p in range(8):
                        w = [("SG", sg_setup), ("DV", hilo_dv[1]),
                             ("AC", pb01f[p % 2])]
                        E("tensor", lambda e, p=p: e.matmul(
                            PB[p % 2][:, 0:512], ombd_sb[:],
                            kT[:, p * 512:(p + 1) * 512],
                            start=True, stop=False,
                            skip_group_check=True),
                          waits=w)
                        E("tensor", lambda e, p=p: e.matmul(
                            PB[p % 2][:, 0:512],
                            shF_sb[:, p * 128:(p + 1) * 128],
                            nrTk[0:32, :], start=False, stop=True,
                            skip_group_check=True),
                          inc="PE")
                        pe_phi = V("PE")
                        E("scalar", lambda e, p=p:
                          e.activation(
                              phikT[:, p * 512:(p + 1) * 512],
                              PB[p % 2][:, 0:512], AF.Exp,
                              bias=negl8_sb[:, 0:1]),
                          waits=[("PE", pe_phi), ("DV", dv_ones)],
                          inc="AC")
                        E("scalar", lambda e, p=p:
                          e.activation(
                              phikTo[:, p * 512:(p + 1) * 512],
                              PB[p % 2][64:128, 0:512], AF.Exp,
                              bias=negl8_sb[0:64, 0:1]),
                          inc="AC")
                        pb01f[p % 2] = V("AC")
                    if STAGE < 4:
                        xnT2_g = 0
                        continue
                    for ts in range(4):
                        for oh in range(2):
                            for Dc in range(8):
                                w = []
                                if Dc == 0 and oh == 0:
                                    w = [("DV", pb45f), ("SG", sg_w)]
                                E("tensor", lambda e, ts=ts, oh=oh, Dc=Dc:
                                  e.matmul(
                                      PB[4 + oh][:, 0:512],
                                      xnT2[:, Dc * 512 + ts * 128:
                                           Dc * 512 + (ts + 1) * 128],
                                      wbf["wv"][:, Dc * 1024 + oh * 512:
                                                Dc * 1024 +
                                                (oh + 1) * 512],
                                      start=(Dc == 0), stop=(Dc == 7),
                                      skip_group_check=True),
                                  waits=w,
                                  inc=("PE" if Dc == 7 else None))
                        pe_v = V("PE")
                        cl = T * 4 + ts
                        for bank in range(2):
                            dst = vp.ap().rearrange(
                                "p (cc h j) -> p cc h j", cc=4, h=16)[
                                :, cl, bank * 8:(bank + 1) * 8, 0:64]
                            src = PB[4 + bank].ap().rearrange(
                                "p (h j) -> p h j", h=8)
                            E("vector", lambda e, dst=dst, src=src:
                              e.tensor_copy(dst, src),
                              waits=[("PE", pe_v), ("DV", dv_vp0)],
                              inc="DV")
                        pb45f = V("DV")
                    xnT2_g = V("PE")
                    if STAGE < 5:
                        continue
                    pb67_ac = V("AC")
                    for ts in range(4):
                        cl = T * 4 + ts
                        ptk_order = [0, 2, 4, 6, 8, 10, 12, 14,
                                     1, 3, 5, 7, 9, 11, 13, 15]
                        for oi, hh in enumerate(ptk_order):
                            bank = 6 + hh // 8
                            kk = kT if hh % 2 == 0 else kTo
                            w = []
                            if oi == 0:
                                w = [("AC", pb67_ac),
                                     ("DV", hilo_dv[1])]
                            E("tensor", lambda e, hh=hh, bank=bank,
                              kk=kk, ts=ts: e.matmul(
                                  PB[bank][:, (hh % 8) * 64:
                                           (hh % 8) * 64 + 64],
                                  kk[0:64,
                                     (hh // 2) * 512 + ts * 128:
                                     (hh // 2) * 512 +
                                     (ts + 1) * 128],
                                  ombd_sb[0:64, 0:64],
                                  start=(hh == 0 or hh == 8), stop=False,
                                  skip_group_check=True),
                              waits=w, inc="PE")
                        for bank in range(2):
                            E("tensor", lambda e, ts=ts, bank=bank:
                              e.matmul(
                                  PB[bank + 6][:, 0:512],
                                  nrTk[:, ts * 128:(ts + 1) * 128],
                                  shT_sb[:, bank * 512:(bank + 1) * 512],
                                  start=False, stop=True,
                                  skip_group_check=True),
                              inc="PE")
                        pe_pt = V("PE")
                        if DEBUG and s == 0 and qtr == 0 and ts == 0:
                            E("gpsimd", lambda e: e.dma_start(
                                dbg_nr.ap(), nrTk[:]),
                              waits=[("DV", hilo_dv[1])],
                              inc="SG", incn=16)
                        for bank in range(2):
                            E("scalar", lambda e, cl=cl, bank=bank:
                              e.activation(
                                  phikt[:, cl * 1024 + bank * 512:
                                        cl * 1024 + (bank + 1) * 512],
                                  PB[bank + 6][:, 0:512], AF.Exp,
                                  bias=negl8_sb[:, 0:1]),
                              waits=[("PE", pe_pt)], inc="AC")
                        pb67_ac = V("AC")
                ac_phi = V("AC")
                if STAGE < 6:
                    continue
                if DEBUG:
                    E("gpsimd", lambda e, s=s, qtr=qtr: e.dma_start(
                        dbg_pt.ap()[s * 4 + qtr], phikt[:]),
                      waits=[("AC", ac_phi)], inc="SG", incn=16)

                pb01s = V("DV")
                pb23f = [("DV", V("DV")), ("DV", V("DV"))]
                pb45f = V("DV")
                pb6f = V("DV")
                pb7f = V("DV")
                for cl in range(4):
                    c = qtr * 4 + cl
                    slotc = cl % 2
                    dvm = [0, 0]
                    for w2_ in range(2):
                        for hh in range(8):
                            h = w2_ * 8 + hh
                            pk_ = phikT if h % 2 == 0 else phikTo
                            pq_ = phiqT if h % 2 == 0 else phiqTo
                            pcol = h // 2
                            w = []
                            if hh == 0:
                                w = [("AC", ac_phi), ("DV", pb01s)]
                            E("tensor", lambda e, pk_=pk_, pq_=pq_,
                              pcol=pcol, hh=hh, cl=cl: e.matmul(
                                  PB[hh % 2][:, (hh // 2) * 128:
                                             (hh // 2) * 128 + 128],
                                  pk_[0:64,
                                      pcol * 512 + cl * 128:
                                      pcol * 512 + (cl + 1) * 128],
                                  pq_[0:64,
                                      pcol * 512 + cl * 128:
                                      pcol * 512 + (cl + 1) * 128],
                                  start=(hh < 2), stop=(hh >= 6),
                                  skip_group_check=True),
                              waits=w, inc="PE")
                        pe_w = V("PE")
                        for par in range(2):
                            d0 = (w2_ * 8 + par * 4) * 128
                            E("vector", lambda e, par=par, d0=d0:
                              e.tensor_tensor(
                                  BT[:, d0:d0 + 512], PB[par][:, 0:512],
                                  mask_sb[:], ALU.mult),
                              waits=[("PE", pe_w)], inc="DV")
                        pb01s = V("DV")
                        dvm[w2_] = V("DV")
                    for p in range(8):
                        nb = 2 + p % 2
                        fw = [("DV", dvm[p // 4]), ("DV", ssb_ev),
                              pb23f[p % 2]]
                        if c > 0:
                            E("tensor", lambda e, p=p, nb=nb, cl=cl:
                              e.matmul(
                                  PB[nb][:, 0:65],
                                  phiqT[0:64, p * 512 + cl * 128:
                                        p * 512 + (cl + 1) * 128],
                                  Ssb[:, (2 * p) * 65:(2 * p + 1) * 65],
                                  start=True, stop=False,
                                  skip_group_check=True),
                              waits=fw)
                            fw = []
                        E("tensor", lambda e, p=p, nb=nb, cl=cl, c=c:
                          e.matmul(
                              PB[nb][:, 0:65],
                              BT[:, bcol(2 * p) * 128:
                                 (bcol(2 * p) + 1) * 128],
                              vp[:, cl * 1040 + (2 * p) * 65:
                                 cl * 1040 + (2 * p) * 65 + 65],
                              start=(c == 0), stop=False,
                              skip_group_check=True),
                          waits=fw, inc="PE")
                        if c > 0:
                            E("tensor", lambda e, p=p, nb=nb, cl=cl:
                              e.matmul(
                                  PB[nb][:, 65:130],
                                  phiqTo[0:64, p * 512 + cl * 128:
                                         p * 512 + (cl + 1) * 128],
                                  Ssb[:, (2 * p + 1) * 65:
                                      (2 * p + 2) * 65],
                                  start=False, stop=False,
                                  skip_group_check=True))
                        E("tensor", lambda e, p=p, nb=nb, cl=cl, c=c:
                          e.matmul(
                              PB[nb][:, 65:130],
                              BT[:, bcol(2 * p + 1) * 128:
                                 (bcol(2 * p + 1) + 1) * 128],
                              vp[:, cl * 1040 + (2 * p + 1) * 65:
                                 cl * 1040 + (2 * p + 1) * 65 + 65],
                              start=False, stop=True,
                              skip_group_check=True),
                          inc="PE")
                        pe_num = V("PE")
                        zq = (p % 2) * 4
                        E("vector", lambda e, nb=nb, zq=zq:
                          e.tensor_scalar(
                            zr[:, zq:zq + 2], PB[nb][:, 64:130:65], 1e-6,
                            None, ALU.add),
                          waits=[("PE", pe_num), ("AC", zr_free[p % 2])],
                          inc="DV")
                        E("vector", lambda e, zq=zq: e.reciprocal(
                            zr[:, zq + 2:zq + 4], zr[:, zq:zq + 2]),
                          waits=[("DV", V("DV"))], inc="DV")
                        dv_z = V("DV")
                        w_at = [("DV", dv_z)]
                        if p == 0:
                            w_at.append(("PE", attn_free))
                        E("scalar", lambda e, nb=nb, p=p, zq=zq:
                          e.activation(
                              attn[:, p * 128:p * 128 + 64],
                              PB[nb][:, 0:64], AF.Copy,
                              scale=zr[:, zq + 2:zq + 3]),
                          waits=w_at, inc="AC")
                        E("scalar", lambda e, nb=nb, p=p, zq=zq:
                          e.activation(
                              attn[:, p * 128 + 64:(p + 1) * 128],
                              PB[nb][:, 65:129], AF.Copy,
                              scale=zr[:, zq + 3:zq + 4]),
                          inc="AC")
                        pb23f[p % 2] = ("AC", V("AC"))
                        zr_free[p % 2] = V("AC")
                    dv_attn = V("AC")
                    for h in range(16):
                        sb_ = 4 + h // 8
                        w = []
                        if h == 0:
                            w = [("DV", ssb_ev), ("AC", ac_phi),
                                 ("DV", pb45f)]
                        E("tensor", lambda e, h=h, sb_=sb_, cl=cl:
                          e.matmul(
                              PB[sb_][0:64,
                                      (h % 8) * 64:(h % 8) * 64 + 64],
                              phikt[:, cl * 1024 + h * 64:
                                    cl * 1024 + (h + 1) * 64],
                              vp[:, cl * 1040 + h * 65:
                                 cl * 1040 + h * 65 + 64],
                              start=(cl == 0 and h % 8 == 0),
                              stop=(cl == 3), skip_group_check=True),
                          waits=w, inc="PE")
                    for h in range(16):
                        w = []
                        if h == 0 and cl == 0:
                            w = [("DV", pb7f)]
                        E("tensor", lambda e, h=h, cl=cl: e.matmul(
                            PB[7][0:64, h:h + 1],
                            phikt[:, cl * 1024 + h * 64:
                                  cl * 1024 + (h + 1) * 64],
                            vp[:, cl * 1040 + h * 65 + 64:
                               cl * 1040 + h * 65 + 65],
                            start=(cl == 0 and h == 0), stop=(cl == 3),
                            skip_group_check=True),
                          waits=w, inc="PE")
                    pe_su = V("PE")
                    qp = qtr % 2
                    for bank in range(2):
                        dst = Ssb.ap().rearrange(
                            "p (h j) -> p h j", h=16)[
                            :, bank * 8:(bank + 1) * 8, 0:64]
                        E("vector", lambda e, bank=bank, qp=qp, dst=dst:
                          e.tensor_tensor(
                              dst, PB[4 + bank][0:64, 0:512],
                              kvb.ap().rearrange(
                                  "p (b n) -> p b n", b=2)[
                                  :, qp, bank * 512:(bank + 1) * 512],
                              ALU.add),
                          waits=[("PE", pe_su)], inc="DV")
                    dstd = Ssb.ap().rearrange("p (h j) -> p h j", h=16)[
                        :, :, 64:65]
                    E("vector", lambda e, qp=qp, dstd=dstd:
                      e.tensor_tensor(
                          dstd,
                          PB[7].ap()[0:64, 0:16].rearrange(
                              "p (h o) -> p h o", h=16),
                          dnb.ap()[0:64, qp * 16:(qp + 1) * 16].rearrange(
                              "p (h o) -> p h o", h=16),
                          ALU.add),
                      waits=[("PE", pe_su)], inc="DV")
                    if cl == 3:
                        E("vector", lambda e, qp=qp: e.tensor_tensor(
                            kvb[0:64, (1 - qp) * 1024:
                                (1 - qp) * 1024 + 512],
                            PB[4][0:64, 0:512],
                            kvb[0:64, qp * 1024:qp * 1024 + 512],
                            ALU.add),
                          waits=[("DV", V("DV"))], inc="DV")
                        E("vector", lambda e, qp=qp: e.tensor_tensor(
                            kvb[0:64, (1 - qp) * 1024 + 512:
                                (1 - qp) * 1024 + 1024],
                            PB[5][0:64, 0:512],
                            kvb[0:64, qp * 1024 + 512:(qp + 1) * 1024],
                            ALU.add),
                          inc="DV")
                        E("vector", lambda e, qp=qp: e.tensor_tensor(
                            dnb[0:64, (1 - qp) * 16:(2 - qp) * 16],
                            PB[7][0:64, 0:16],
                            dnb[0:64, qp * 16:(qp + 1) * 16],
                            ALU.add),
                          inc="DV")
                    ssb_ev = V("DV")
                    pb45f = ssb_ev
                    if DEBUG:
                        E("gpsimd", lambda e, s=s, qtr=qtr, cl=cl:
                          e.dma_start(
                              dbg_ss.ap()[(s * 4 + qtr) * 4 + cl],
                              Ssb[:]),
                          waits=[("DV", ssb_ev)], inc="SG", incn=16)
                    gi = cl % 4
                    for w2_ in range(2):
                        for j in range(4):
                            w = ([("AC", dv_attn), ("DV", pb6f)]
                                 if j == 0 else [])
                            E("tensor", lambda e, w2_=w2_,
                              j=j: e.transpose(
                                  PB[6][:, j * 128:(j + 1) * 128],
                                  attn[:, (w2_ * 4 + j) * 128:
                                       (w2_ * 4 + j) * 128 + 128],
                                  ident_sb[:]),
                              waits=w, inc="PE")
                        pe_tr = V("PE")
                        attn_free = pe_tr
                        dst = attnT.ap().rearrange(
                            "p (gi ic c) -> p gi ic c", gi=4, ic=8)[
                            :, gi, w2_ * 4:(w2_ + 1) * 4, :]
                        src = PB[6].ap().rearrange(
                            "p (ic c) -> p ic c", ic=4)
                        E("vector", lambda e, dst=dst, src=src:
                          e.tensor_copy(dst, src),
                          waits=[("PE", pe_tr)], inc="DV")
                        pb6f = V("DV")
                    attnT_rdy = V("DV")
                    if STAGE >= 7 and cl % 4 == 3:
                        for gi2 in range(4):
                            tg = qtr * 512 + (cl - 3 + gi2) * 128
                            xs = gi2 % 2
                            E("gpsimd", lambda e, s=s, tg=tg, xs=xs:
                              e.dma_start(
                                  x2[:, xs * 1024:(xs + 1) * 1024],
                                  x_d.ap()[s, tg:tg + 128, :]),
                              waits=[("DV", x2f[xs])], inc="SG", incn=16)
                            E("gpsimd", lambda e, s=s, tg=tg, xs=xs:
                              e.dma_start(
                                  g2[:, xs * 1024:(xs + 1) * 1024],
                                  g_s.ap()[s, tg:tg + 128, :]),
                              waits=[("DV", g2f[xs])], inc="SG", incn=16)
                            sx_c = V("SG")
                            os_ = 0
                            for eh in range(2):
                                for ic in range(8):
                                    w = []
                                    if ic == 0:
                                        w = [("DV", attnT_rdy),
                                             ("DV", pb7f)]
                                    E("tensor", lambda e, gi2=gi2, ic=ic,
                                      eh=eh: e.matmul(
                                          PB[7][:, 0:512],
                                          attnT[:, gi2 * 1024 +
                                                ic * 128:
                                                gi2 * 1024 +
                                                (ic + 1) * 128],
                                          wbf["wo"][:, ic * 1024 +
                                                    eh * 512:
                                                    ic * 1024 +
                                                    (eh + 1) * 512],
                                          start=(ic == 0), stop=False,
                                          skip_group_check=True),
                                      waits=w)
                                E("tensor", lambda e, eh=eh: e.matmul(
                                    PB[7][:, 0:512], ones2_sb[:],
                                    bohl_sb[:, eh * 512:(eh + 1) * 512],
                                    start=False, stop=True,
                                    skip_group_check=True),
                                  inc="PE")
                                pe_wo = V("PE")
                                E("vector", lambda e, xs=xs, eh=eh:
                                  e.tensor_tensor(
                                      sg[:], PB[7][:, 0:512],
                                      g2[:, xs * 1024 + eh * 512:
                                         xs * 1024 + (eh + 1) * 512],
                                      ALU.mult),
                                  waits=[("PE", pe_wo), ("SG", sx_c)],
                                  inc="DV")
                                E("vector", lambda e, xs=xs, eh=eh,
                                  os_=os_: e.tensor_tensor(
                                      outb[:, os_ * 1024 + eh * 512:
                                           os_ * 1024 +
                                           (eh + 1) * 512],
                                      sg[:],
                                      x2[:, xs * 1024 + eh * 512:
                                         xs * 1024 + (eh + 1) * 512],
                                      ALU.add),
                                  waits=[("DV", V("DV")),
                                         ("SC", outb_st[os_])],
                                  inc="DV")
                                pb7f = V("DV")
                            dv_out = V("DV")
                            E("scalar", lambda e, s=s, tg=tg, os_=os_:
                              e.dma_start(
                                  y_d.ap()[s, tg:tg + 128, :],
                                  outb[:, os_ * 1024:(os_ + 1) * 1024]),
                              waits=[("DV", dv_out)], inc="SC", incn=16)
                            outb_st[os_] = V("SC")
                            x2f[xs] = dv_out
                            g2f[xs] = dv_out

        if DEBUG:
            prog.barrier()
            E("sync", lambda e: e.dma_start(dbg_xnt.ap(), xnT_s.ap()),
              inc="SX", incn=16)
            E("sync", lambda e: e.dma_start(dbg_g.ap(), g_s.ap()),
              inc="SX", incn=16)
        prog.barrier()
        prog.build()

    return nc


def _host_consts():
    ident = np.eye(128, dtype=np.float32)
    mask = np.ascontiguousarray(
        np.tile(np.triu(np.ones((128, 128), dtype=np.float32)), (1, 4)))
    seln = np.zeros((128, 128), dtype=np.float32)
    for p in range(8):
        seln[0:64, p * 16 + 2 * p] = -0.5
        seln[64:128, p * 16 + 2 * p + 1] = -0.5
    shF = np.zeros((32, 1024), dtype=np.float32)
    for j in range(32):
        hsel = j % 16
        p = hsel // 2
        if hsel % 2 == 0:
            shF[j, p * 128:p * 128 + 64] = 1.0
        else:
            shF[j, p * 128 + 64:(p + 1) * 128] = 1.0
    shT = np.zeros((32, 1024), dtype=np.float32)
    for j in range(32):
        shT[j, (j % 16) * 64:((j % 16) + 1) * 64] = 1.0
    shFo = np.zeros((128, 512), dtype=np.float32)
    for j in range(32):
        hsel = j % 16
        if hsel % 2 == 1:
            p = hsel // 2
            shFo[j, p * 64:(p + 1) * 64] = 1.0
    hsel = np.zeros((128, 128), dtype=np.float32)
    hsel[0:64, 0:64] = -0.5
    hsel[64:128, 64:128] = -0.5
    return ident, mask, seln, shF, shT, shFo, hsel


def kernel(L0_attn_x, L0_attn_omega, L0_attn_wq, L0_attn_wk, L0_attn_wv,
           L0_attn_wo, L0_attn_bo, L0_ffn_W1, L0_ffn_b1, L0_ffn_W2,
           L0_ffn_b2):
    x = np.asarray(L0_attn_x, dtype=np.float32)
    ident, mask, seln, shF, shT, shFo, hsel = _host_consts()
    omT = np.ascontiguousarray(np.asarray(L0_attn_omega, np.float32).T)
    ombd = np.zeros((128, 128), dtype=np.float32)
    ombd[0:64, 0:64] = omT
    ombd[64:128, 64:128] = omT
    common = dict(
        wqT=np.ascontiguousarray(np.asarray(L0_attn_wq, np.float32).T),
        wkT=np.ascontiguousarray(np.asarray(L0_attn_wk, np.float32).T),
        wvT=np.ascontiguousarray(np.asarray(L0_attn_wv, np.float32).T),
        woT=np.ascontiguousarray(np.asarray(L0_attn_wo, np.float32).T),
        ombd=ombd,
        w1T=np.ascontiguousarray(np.asarray(L0_ffn_W1, np.float32).T),
        w2T=np.ascontiguousarray(np.asarray(L0_ffn_W2, np.float32).T),
        b1c=np.ascontiguousarray(
            np.asarray(L0_ffn_b1, np.float32).reshape(32, 128).T),
        b2c=np.ascontiguousarray(
            np.asarray(L0_ffn_b2, np.float32).reshape(8, 128).T),
        bo_row=np.asarray(L0_attn_bo, np.float32).reshape(1, D),
        ident=ident, mask4=mask, seln=seln, selhlF=shF, selhlT=shT,
        halfsel=hsel,
    )
    nc = build_nc()
    in_maps = []
    for c in range(NCORES):
        m = dict(common)
        m["x"] = np.ascontiguousarray(x[2 * c:2 * c + 2])
        in_maps.append(m)
    res = run_bass_kernel_spmd(nc, in_maps, core_ids=list(range(NCORES)))
    out = np.concatenate([r["y"].reshape(SEQ, N, D) for r in res.results],
                         axis=0)
    return np.ascontiguousarray(out, dtype=np.float32)


# version 18
# speedup vs baseline: 1.0454x; 1.0036x over previous
import contextlib
import math
import numpy as np
import concourse.bass as bass
import concourse.mybir as mybir
from concourse.bass_utils import run_bass_kernel_spmd

F32 = mybir.dt.float32
BF16 = mybir.dt.bfloat16
AF = mybir.ActivationFunctionType
ALU = mybir.AluOpType
AX = mybir.AxisListType

NCORES = 8
SEQ = 2
N = 2048
D = 1024
FF = 4096
LN8 = math.log(8.0)
TT1 = 256
NT1 = N // TT1
ENGINES = ("sync", "gpsimd", "scalar", "vector", "tensor")


class Prog:

    def __init__(self, nc):
        self.nc = nc
        self.items = {e: [] for e in ENGINES}
        self.sems = {}
        self.pending = {e: [] for e in ENGINES}

    def sem(self, name):
        self.sems[name] = [self.nc.alloc_semaphore(name), 0]

    def val(self, name):
        return self.sems[name][1]

    def emit(self, eng, fn, waits=(), inc=None, incn=1):
        wl = []
        for (s, v) in waits:
            assert v <= self.sems[s][1], (s, v, self.sems[s][1])
            if v > 0:
                wl.append((self.sems[s][0], v))
        if self.pending[eng]:
            wl = self.pending[eng] + wl
            self.pending[eng] = []
        handle = None
        if inc is not None:
            self.sems[inc][1] += incn
            handle = self.sems[inc][0]
        self.items[eng].append((fn, wl, handle, incn))

    def barrier(self):
        snap = [(self.sems[s][0], self.sems[s][1]) for s in self.sems
                if self.sems[s][1] > 0]
        for e in ENGINES:
            self.pending[e] = self.pending[e] + list(snap)

    def build(self):
        with self.nc.Block() as blk:
            for engname in ENGINES:
                items = self.items[engname]
                tail = self.pending[engname]
                if not items and not tail:
                    continue

                def body(eng, items=items, tail=tail):
                    for fn, wl, handle, incn in items:
                        for h, v in wl:
                            eng.wait_ge(h, v)
                        instr = fn(eng)
                        if handle is not None:
                            instr.then_inc(handle, incn)
                    for h, v in tail:
                        eng.wait_ge(h, v)

                getattr(blk, engname)(body)
        self.items = {e: [] for e in ENGINES}
        self.pending = {e: [] for e in ENGINES}


def bcol(h):
    return (h // 8) * 8 + (h % 2) * 4 + (h % 8) // 2


DEBUG = False
STAGE = 99


def build_nc():
    nc = bass.Bass("TRN2", target_bir_lowering=False,
                   detect_race_conditions=False)

    x_d = nc.dram_tensor("x", [SEQ, N, D], F32, kind="ExternalInput")
    wT_d = {k: nc.dram_tensor(k, [D, D], F32, kind="ExternalInput")
            for k in ("wqT", "wkT", "wvT", "woT")}
    ombd_d = nc.dram_tensor("ombd", [128, 128], F32, kind="ExternalInput")
    w1T_d = nc.dram_tensor("w1T", [D, FF], F32, kind="ExternalInput")
    w2T_d = nc.dram_tensor("w2T", [FF, D], F32, kind="ExternalInput")
    b1c_d = nc.dram_tensor("b1c", [128, 32], F32, kind="ExternalInput")
    b2c_d = nc.dram_tensor("b2c", [128, 8], F32, kind="ExternalInput")
    bo_d = nc.dram_tensor("bo_row", [1, D], F32, kind="ExternalInput")
    ident_d = nc.dram_tensor("ident", [128, 128], F32, kind="ExternalInput")
    mask_d = nc.dram_tensor("mask4", [128, 512], F32, kind="ExternalInput")
    seln_d = nc.dram_tensor("seln", [128, 128], F32, kind="ExternalInput")
    hsel_d = nc.dram_tensor("halfsel", [128, 128], F32,
                            kind="ExternalInput")
    shF_d = nc.dram_tensor("selhlF", [32, 1024], F32, kind="ExternalInput")
    shT_d = nc.dram_tensor("selhlT", [32, 1024], F32, kind="ExternalInput")
    xnT_s = nc.dram_tensor("xnT_s", [SEQ, 8, 128, N], BF16, kind="Internal")
    g_s = nc.dram_tensor("g_s", [SEQ, N, D], F32, kind="Internal")
    bohl_scr = nc.dram_tensor("bohl_scr", [2, 1024], BF16, kind="Internal")
    y_d = nc.dram_tensor("y", [SEQ, N, D], F32, kind="ExternalOutput")
    if DEBUG:
        dbg_xnt = nc.dram_tensor("dbg_xnt", [SEQ, 8, 128, N], BF16,
                                 kind="ExternalOutput")
        dbg_g = nc.dram_tensor("dbg_g", [SEQ, N, D], F32,
                               kind="ExternalOutput")
        dbg_pt = nc.dram_tensor("dbg_pt", [8, 128, 4096], F32,
                                kind="ExternalOutput")
        dbg_w1 = nc.dram_tensor("dbg_w1", [128, 8 * FF], BF16,
                                kind="ExternalOutput")
        dbg_w2 = nc.dram_tensor("dbg_w2", [128, 32 * D], BF16,
                                kind="ExternalOutput")
        dbg_h = nc.dram_tensor("dbg_h", [128, 2 * 256], BF16,
                               kind="ExternalOutput")
        dbg_nr = nc.dram_tensor("dbg_nr", [32, 512], F32,
                                kind="ExternalOutput")
        dbg_ss = nc.dram_tensor("dbg_ss", [32, 64, 1040], F32,
                                kind="ExternalOutput")

    A = nc.alloc_sbuf_tensor
    ident_sb = A("ident_sb", [128, 128], F32)
    mask_sb = A("mask_sb", [128, 512], BF16)
    seln_sb = A("seln_sb", [128, 128], F32)
    hsel_sb = A("hsel_sb", [128, 128], F32)
    shF_sb = A("shF_sb", [32, 1024], BF16)
    shT_sb = A("shT_sb", [32, 1024], BF16)
    ombd_sb = A("ombd_sb", [128, 128], BF16)
    b1c_sb = A("b1c_sb", [128, 32], F32)
    b2c_sb = A("b2c_sb", [128, 8], F32)
    bohl_sb = A("bohl_sb", [2, D], BF16)
    ones2_sb = A("ones2_sb", [2, 128], BF16)
    onesc_sb = A("onesc_sb", [128, 1], F32)
    onesr_sb = A("onesr_sb", [1, 128], F32)
    negl8_sb = A("negl8_sb", [128, 1], F32)
    eps_sb = A("eps_sb", [128, 1], F32)

    PB = [nc.alloc_psum_tensor(f"PB{i}", [128, 512], F32) for i in range(8)]

    prog = Prog(nc)
    for s in ("SX", "SY", "SC", "SG", "PE", "DV", "AC"):
        prog.sem(s)
    E = prog.emit
    V = prog.val

    for dst, src in ((ident_sb, ident_d), (seln_sb, seln_d),
                     (b1c_sb, b1c_d), (b2c_sb, b2c_d),
                     (hsel_sb, hsel_d)):
        E("sync", lambda e, d=dst, s=src: e.dma_start(d[:], s.ap()),
          inc="SX", incn=16)
    sx_setup = V("SX")
    for dst, src in ((ombd_sb, ombd_d), (shF_sb, shF_d), (shT_sb, shT_d),
                     (mask_sb, mask_d)):
        E("gpsimd", lambda e, d=dst, s=src: e.dma_start(d[:], s.ap()),
          inc="SG", incn=16)
    sg_setup = V("SG")
    E("vector", lambda e: e.memset(ones2_sb[:], 1.0), inc="DV")
    E("vector", lambda e: e.memset(onesc_sb[:], 1.0), inc="DV")
    E("vector", lambda e: e.memset(onesr_sb[:], 1.0), inc="DV")
    E("vector", lambda e: e.memset(negl8_sb[:], -LN8), inc="DV")
    E("vector", lambda e: e.memset(eps_sb[:], 1e-5), inc="DV")
    dv_ones = V("DV")

    with contextlib.ExitStack() as st1:
        S1 = lambda name, shape, dt: st1.enter_context(
            nc.sbuf_tensor(name, shape, dt))
        bor_sb = S1("bor_sb", [128, 8], F32)
        bohi_b = S1("bohi_b", [128, 8], BF16)
        bohi_f = S1("bohi_f", [128, 8], F32)
        bolo_f = S1("bolo_f", [128, 8], F32)
        bolo_b = S1("bolo_b", [128, 8], BF16)
        w1bf = S1("w1bf", [128, 8 * FF], BF16)
        w2bf = S1("w2bf", [128, 32 * D], BF16)
        wst = S1("wst", [128, 2 * 1024], F32)
        wst2 = S1("wst2", [128, 2 * 1024], F32)
        tn = S1("tn", [128, 1024], BF16)
        tn2 = S1("tn2", [128, 1024], BF16)
        acc = S1("acc", [128, 4], F32)
        acc2 = S1("acc2", [128, 4], F32)
        inv1 = S1("inv1", [128, 1], F32)
        inv2 = S1("inv2", [128, 1], F32)
        sc1 = S1("sc1", [1, 4], F32)
        x1 = S1("x1", [128, 3 * 1024], F32)
        scr = S1("scr", [128, 2 * 1024], F32)
        lnc = S1("lnc", [128, 16], F32)
        xnT1 = S1("xnT1", [128, 2 * 2048], BF16)
        hbf = S1("hbf", [128, 2 * 256], BF16)
        gTr = S1("gTr", [128, 2 * 256], F32)
        gtok = S1("gtok", [128, 2048], F32)

        E("sync", lambda e: e.dma_start(
            bor_sb[:], bo_d.ap().rearrange("o (p c) -> (o p) c", p=128)),
          inc="SX", incn=16)
        sx_bo = V("SX")
        E("vector", lambda e: e.tensor_copy(bohi_b[:], bor_sb[:]),
          waits=[("SX", sx_bo), ("DV", dv_ones)], inc="DV")
        E("vector", lambda e: e.tensor_copy(bohi_f[:], bohi_b[:]),
          waits=[("DV", V("DV"))], inc="DV")
        E("vector", lambda e: e.tensor_tensor(bolo_f[:], bor_sb[:],
                                              bohi_f[:], ALU.subtract),
          waits=[("DV", V("DV"))], inc="DV")
        E("vector", lambda e: e.tensor_copy(bolo_b[:], bolo_f[:]),
          waits=[("DV", V("DV"))], inc="DV")
        dv_bolo = V("DV")
        E("sync", lambda e: e.dma_start(
            bohl_scr.ap()[0:1, :].rearrange("o (p c) -> (o p) c", p=128),
            bohi_b[:]),
          waits=[("DV", dv_bolo)], inc="SX", incn=16)
        E("sync", lambda e: e.dma_start(
            bohl_scr.ap()[1:2, :].rearrange("o (p c) -> (o p) c", p=128),
            bolo_b[:]),
          waits=[("DV", dv_bolo)], inc="SX", incn=16)
        sx_bscr = V("SX")
        E("sync", lambda e: e.dma_start(bohl_sb[:], bohl_scr.ap()),
          waits=[("SX", sx_bscr)], inc="SX", incn=16)

        mats = [
            dict(mat=w1T_d, nco=FF, dst=w1bf, inv=inv1, stg=wst,
                 accb=acc, tnb=tn, bank=0, scc=0, per=4, dq="SX"),
            dict(mat=w2T_d, nco=D, dst=w2bf, inv=inv2, stg=wst2,
                 accb=acc2, tnb=tn2, bank=1, scc=2, per=1, dq="SY"),
        ]
        for m in mats:
            E("vector", lambda e, m=m: e.memset(m["accb"][:, 1:2], 0.0),
              inc="DV")
        dv_ms = V("DV")
        tguard = [[dv_ms, dv_ms], [dv_ms, dv_ms]]
        for k in range(32):
            for i, m in enumerate(mats):
                rk, ck = k // m["per"], k % m["per"]
                slot = k % 2
                src = m["mat"].ap()[rk * 128:(rk + 1) * 128,
                                    ck * 1024:(ck + 1) * 1024]
                E("sync", lambda e, s=src, m=m, sl=slot: e.dma_start(
                    m["stg"][:, sl * 1024:(sl + 1) * 1024], s),
                  waits=[("DV", tguard[i][slot])], inc=m["dq"], incn=16)
                sxv = V(m["dq"])
                E("vector", lambda e, m=m, sl=slot: e.reduce_sum(
                    m["accb"][:, 2:3],
                    m["stg"][:, sl * 1024:(sl + 1) * 1024],
                    axis=AX.X, apply_absolute_value=True),
                  waits=[(m["dq"], sxv)], inc="DV")
                dv_red = V("DV")
                E("vector", lambda e, m=m, k=k: e.tensor_add(
                    m["accb"][:, k % 2:k % 2 + 1],
                    m["accb"][:, (k + 1) % 2:(k + 1) % 2 + 1],
                    m["accb"][:, 2:3]),
                  waits=[("DV", dv_red)], inc="DV")
                tguard[i][slot] = V("DV")
        dv_invc = [0, 0]
        for i, m in enumerate(mats):
            c_acc = tguard[i][1 - 32 % 2]
            E("tensor", lambda e, m=m: e.matmul(
                PB[m["bank"]][0:1, 0:1], m["accb"][:, 1:2], onesc_sb[:],
                start=True, stop=True, skip_group_check=True),
              waits=[("DV", c_acc)], inc="PE")
            pe_tot = V("PE")
            E("vector", lambda e, m=m: e.tensor_scalar(
                sc1[:, m["scc"]:m["scc"] + 1], PB[m["bank"]][0:1, 0:1],
                1.0 / (D * FF), 1e-8, ALU.mult, ALU.max),
              waits=[("PE", pe_tot)], inc="DV")
            E("vector", lambda e, m=m: e.reciprocal(
                sc1[:, m["scc"] + 1:m["scc"] + 2],
                sc1[:, m["scc"]:m["scc"] + 1]),
              waits=[("DV", V("DV"))], inc="DV")
            dv_inv = V("DV")
            E("tensor", lambda e, m=m: e.matmul(
                PB[m["bank"]][:, 0:1], onesr_sb[:],
                sc1[:, m["scc"] + 1:m["scc"] + 2],
                start=True, stop=True, skip_group_check=True),
              waits=[("DV", dv_inv)], inc="PE")
            pe_bc = V("PE")
            E("vector", lambda e, m=m: e.tensor_copy(
                m["inv"][:], PB[m["bank"]][:, 0:1]),
              waits=[("PE", pe_bc)], inc="DV")
            dv_invc[i] = V("DV")
        tguard = [[V("DV")] * 2, [V("DV")] * 2]
        tdone = [0, 0]
        for k in range(32):
            for i, m in enumerate(mats):
                rk, ck = k // m["per"], k % m["per"]
                slot = k % 2
                src = m["mat"].ap()[rk * 128:(rk + 1) * 128,
                                    ck * 1024:(ck + 1) * 1024]
                E("sync", lambda e, s=src, m=m, sl=slot: e.dma_start(
                    m["stg"][:, sl * 1024:(sl + 1) * 1024], s),
                  waits=[("DV", tguard[i][slot])], inc=m["dq"], incn=16)
                sxv = V(m["dq"])
                d0 = rk * m["nco"] + ck * 1024
                E("vector", lambda e, m=m, sl=slot, d0=d0: e.tensor_scalar(
                    m["dst"][:, d0:d0 + 1024],
                    m["stg"][:, sl * 1024:(sl + 1) * 1024],
                    m["inv"][:, 0:1], 0.5, ALU.mult, ALU.is_gt),
                  waits=[(m["dq"], sxv), ("DV", dv_invc[i])])
                E("vector", lambda e, m=m, sl=slot: e.tensor_scalar(
                    m["tnb"][:], m["stg"][:, sl * 1024:(sl + 1) * 1024],
                    m["inv"][:, 0:1], -0.5, ALU.mult, ALU.is_lt),
                  inc="DV")
                tguard[i][slot] = V("DV")
                E("vector", lambda e, m=m, d0=d0: e.tensor_tensor(
                    m["dst"][:, d0:d0 + 1024], m["dst"][:, d0:d0 + 1024],
                    m["tnb"][:], ALU.subtract),
                  waits=[("DV", V("DV"))], inc="DV")
                tdone[i] = V("DV")
        tern1_dv, tern2_sg = tdone[0], tdone[1]
        if DEBUG:
            E("sync", lambda e: e.dma_start(dbg_w1.ap(), w1bf[:]),
              waits=[("DV", tern1_dv)], inc="SX", incn=16)
            E("sync", lambda e: e.dma_start(dbg_w2.ap(), w2bf[:]),
              waits=[("DV", tern2_sg)], inc="SX", incn=16)

        xg = [0, 0, 0]
        scr_g = [0, 0]
        pb01f = [0, 0]
        out2_free = 0
        pb6_ev = 0
        pb7_ev = 0
        xnt_rd = [0, 0]
        xnt_use = [0, 0]
        gtok_st = 0

        def emit_ffn(s, t0, slot):
            nonlocal out2_free, gtok_st, pb7_ev
            pe_mm2_last = 0
            ac_hv = [0, 0]
            h_free = [0, 0]
            for F in range(33):
                if F < 32:
                    b = F % 2
                    for Dc in range(8):
                        w = []
                        if Dc == 0:
                            w = [("DV", tern1_dv), ("DV", xnt_rd[slot]),
                                 ("AC", pb01f[b])]
                            if F == 0:
                                w += [("AC", out2_free),
                                      ("DV", tern2_sg)]
                        E("tensor", lambda e, F=F, Dc=Dc, b=b, slot=slot:
                          e.matmul(PB[b][:, 0:256],
                                   w1bf[:, Dc * FF + F * 128:
                                        Dc * FF + (F + 1) * 128],
                                   xnT1[:, slot * 2048 + Dc * 256:
                                        slot * 2048 + (Dc + 1) * 256],
                                   start=(Dc == 0), stop=(Dc == 7),
                                   skip_group_check=True),
                          waits=w, inc=("PE" if Dc == 7 else None))
                    pe_mm1 = V("PE")
                if F >= 1:
                    Fm = F - 1
                    hs2 = Fm % 2
                    for Ec in range(8):
                        ob = 2 + Ec // 2
                        oc = (Ec % 2) * 256
                        w = [("AC", ac_hv[hs2])] if Ec == 0 else []
                        E("tensor", lambda e, Fm=Fm, Ec=Ec, ob=ob, oc=oc,
                          hs2=hs2: e.matmul(
                              PB[ob][:, oc:oc + 256],
                              w2bf[:, Fm * D + Ec * 128:
                                   Fm * D + (Ec + 1) * 128],
                              hbf[:, hs2 * 256:(hs2 + 1) * 256],
                              start=(Fm == 0 and Ec % 2 == 0),
                              stop=(Fm == 31), skip_group_check=True),
                          waits=w,
                          inc=("PE" if (Ec == 7 or Fm == 31) else None))
                    pe_mm2_last = V("PE")
                    h_free[hs2] = pe_mm2_last
                if F < 32:
                    hs = F % 2
                    E("scalar", lambda e, F=F, b=b, hs=hs: e.activation(
                        hbf[:, hs * 256:(hs + 1) * 256], PB[b][:, 0:256],
                        AF.Gelu, bias=b1c_sb[:, F:F + 1]),
                      waits=[("PE", pe_mm1), ("PE", h_free[hs])],
                      inc="AC")
                    pb01f[b] = V("AC")
                    ac_hv[hs] = V("AC")
            gtr_free = [0, 0]
            for Ec in range(8):
                ob = 2 + Ec // 2
                oc = (Ec % 2) * 256
                gs = Ec % 2
                E("scalar", lambda e, ob=ob, oc=oc, gs=gs, Ec=Ec:
                  e.activation(gTr[:, gs * 256:(gs + 1) * 256],
                               PB[ob][:, oc:oc + 256], AF.Silu,
                               bias=b2c_sb[:, Ec:Ec + 1]),
                  waits=[("PE", pe_mm2_last), ("PE", gtr_free[gs])],
                  inc="AC")
                ac_g = V("AC")
                for ts in range(2):
                    w = ([("AC", ac_g), ("DV", pb7_ev)] if ts == 0 else [])
                    E("tensor", lambda e, ts=ts, gs=gs: e.transpose(
                        PB[7][:, gs * 256 + ts * 128:
                              gs * 256 + (ts + 1) * 128],
                        gTr[:, gs * 256 + ts * 128:
                            gs * 256 + (ts + 1) * 128],
                        ident_sb[:]),
                      waits=w, inc="PE")
                gtr_free[gs] = V("PE")
                if Ec % 2 == 1:
                    pe_gt = V("PE")
                    e0 = Ec - 1
                    dst = gtok.ap().rearrange(
                        "p (ts ec c) -> p ec ts c", ts=2, ec=8)[
                        :, e0:e0 + 2, :, :]
                    src = PB[7].ap().rearrange(
                        "p (ec ts c) -> p ec ts c", ec=2, ts=2)
                    E("vector", lambda e, dst=dst, src=src:
                      e.tensor_copy(dst, src),
                      waits=[("PE", pe_gt), ("SC", gtok_st)], inc="DV")
                    pb7_ev = V("DV")
            if DEBUG and s == SEQ - 1 and t0 == N - TT1:
                E("sync", lambda e: e.dma_start(dbg_h.ap(), hbf[:]),
                  waits=[("PE", pe_mm2_last)], inc="SX", incn=16)
            out2_free = V("AC")
            dv_g = V("DV")
            E("scalar", lambda e, s=s, t0=t0: e.dma_start(
                g_s.ap()[s, t0:t0 + TT1, :].rearrange(
                    "(ts p) e -> p ts e", p=128),
                gtok.ap().rearrange("p (ts e) -> p ts e", ts=2)),
              waits=[("DV", dv_g)], inc="SC", incn=16)
            gtok_st = V("SC")
            xnt_use[slot] = pe_mm2_last

        for it in range(SEQ * NT1):
            s, T = it // NT1, it % NT1
            t0 = T * TT1
            slot = it % 2
            dv_xn = [0, 0]
            for ts in range(2):
                xs = (2 * it + ts) % 3
                xap = x_d.ap()[s, t0 + ts * 128:t0 + (ts + 1) * 128, :]
                E("sync", lambda e, xap=xap, xs=xs: e.dma_start(
                    x1[:, xs * 1024:(xs + 1) * 1024], xap),
                  waits=[("DV", xg[xs])], inc="SX", incn=16)
                sxv = V("SX")
                c = ts * 8
                E("scalar", lambda e, xs=xs, ts=ts, c=c: e.activation(
                    scr[:, ts * 1024:(ts + 1) * 1024],
                    x1[:, xs * 1024:(xs + 1) * 1024], AF.Square,
                    accum_out=lnc[:, c:c + 1]),
                  waits=[("SX", sxv), ("PE", scr_g[ts])], inc="AC")
                ac_sq = V("AC")
                E("vector", lambda e, xs=xs, c=c: e.reduce_sum(
                    lnc[:, c + 1:c + 2], x1[:, xs * 1024:(xs + 1) * 1024],
                    axis=AX.X),
                  waits=[("SX", sxv)], inc="DV")
                E("vector", lambda e, c=c: e.tensor_scalar(
                    lnc[:, c + 2:c + 3], lnc[:, c + 1:c + 2], 1.0 / D, None,
                    ALU.mult),
                  waits=[("DV", V("DV"))], inc="DV")
                E("vector", lambda e, c=c: e.tensor_tensor(
                    lnc[:, c + 3:c + 4], lnc[:, c + 2:c + 3],
                    lnc[:, c + 2:c + 3], ALU.mult),
                  waits=[("DV", V("DV"))], inc="DV")
                E("vector", lambda e, c=c: e.scalar_tensor_tensor(
                    lnc[:, c + 4:c + 5], lnc[:, c:c + 1], 1.0 / D,
                    lnc[:, c + 3:c + 4], ALU.mult, ALU.subtract),
                  waits=[("DV", V("DV")), ("AC", ac_sq)], inc="DV")
                dv_var = V("DV")
                E("scalar", lambda e, c=c: e.activation(
                    lnc[:, c + 5:c + 6], lnc[:, c + 4:c + 5], AF.Ln,
                    bias=eps_sb[:, 0:1]),
                  waits=[("DV", dv_var), ("DV", dv_ones)], inc="AC")
                E("scalar", lambda e, c=c: e.activation(
                    lnc[:, c + 6:c + 7], lnc[:, c + 5:c + 6], AF.Exp,
                    scale=-0.5),
                  waits=[("AC", V("AC"))], inc="AC")
                ac_rstd = V("AC")
                E("vector", lambda e, xs=xs, ts=ts, c=c: e.tensor_scalar(
                    scr[:, ts * 1024:(ts + 1) * 1024],
                    x1[:, xs * 1024:(xs + 1) * 1024],
                    lnc[:, c + 2:c + 3], lnc[:, c + 6:c + 7],
                    ALU.subtract, ALU.mult),
                  waits=[("AC", ac_rstd)], inc="DV")
                dv_xn[ts] = V("DV")
                xg[xs] = V("DV")
            for ts in range(2):
                for wave in range(2):
                    for j in range(4):
                        cidx = wave * 4 + j
                        w = ([("DV", dv_xn[ts]), ("DV", pb6_ev),
                              ("SX", sx_setup)] if j == 0 else [])
                        E("tensor", lambda e, ts=ts, cidx=cidx, j=j:
                          e.transpose(PB[6][:, j * 128:(j + 1) * 128],
                                      scr[:, ts * 1024 + cidx * 128:
                                          ts * 1024 + (cidx + 1) * 128],
                                      ident_sb[:]),
                          waits=w, inc="PE")
                    pe_tp = V("PE")
                    dst = xnT1.ap()[:, slot * 2048:(slot + 1) * 2048]
                    dst = dst.rearrange("p (dc t) -> p dc t", dc=8)[
                        :, wave * 4:(wave + 1) * 4,
                        ts * 128:(ts + 1) * 128]
                    src4 = PB[6].ap().rearrange("p (dc t) -> p dc t", dc=4)
                    E("vector", lambda e, dst=dst, src4=src4:
                      e.tensor_copy(dst, src4),
                      waits=[("PE", pe_tp), ("PE", xnt_use[slot])],
                      inc="DV")
                    pb6_ev = V("DV")
            scr_g[0] = scr_g[1] = V("PE")
            xnt_rd[slot] = V("DV")
            E("scalar", lambda e, s=s, t0=t0, slot=slot: e.dma_start(
                xnT_s.ap()[s].rearrange("c p t -> p c t")[
                    :, :, t0:t0 + TT1],
                xnT1.ap()[:, slot * 2048:(slot + 1) * 2048].rearrange(
                    "p (c t) -> p c t", c=8)),
              waits=[("DV", xnt_rd[slot])], inc="SC", incn=16)
            if it > 0:
                emit_ffn(*ffn_prev)
            ffn_prev = (s, t0, slot)

        emit_ffn(*ffn_prev)
        prog.build()

    prog.barrier()
    if STAGE <= 1:
        prog.build()
        return nc
    with contextlib.ExitStack() as st2:
        S2 = lambda name, shape, dt: st2.enter_context(
            nc.sbuf_tensor(name, shape, dt))
        wbf = {k: S2(k + "bf", [128, 8 * 1024], BF16)
               for k in ("wq", "wk", "wv", "wo")}
        xnT2 = S2("xnT2", [128, 8 * 512], BF16)
        qT = S2("qT", [128, 8 * 512], BF16)
        kT = S2("kT", [128, 8 * 512], BF16)
        kTo = S2("kTo", [64, 8 * 512], BF16)
        sqp = S2("sqp", [128, 2 * 512], F32)
        nhb = S2("nhb", [128, 16], BF16)
        nhl = S2("nhl", [128, 32], F32)
        nrTk = S2("nrTk", [32, 512], BF16)
        phiqT = S2("phiqT", [128, 8 * 512], BF16)
        phiqTo = S2("phiqTo", [64, 8 * 512], BF16)
        phikT = S2("phikT", [128, 8 * 512], BF16)
        phikTo = S2("phikTo", [64, 8 * 512], BF16)
        phikt = S2("phikt", [128, 4 * 1024], BF16)
        vp = S2("vp", [128, 4 * 1040], BF16)
        Ssb = S2("Ssb", [64, 16 * 65], BF16)
        kvb = S2("kvb", [64, 2 * 1024], F32)
        dnb = S2("dnb", [64, 2 * 16], F32)
        BT = S2("BT", [128, 16 * 128], BF16)
        zr = S2("zr", [128, 8], F32)
        attn = S2("attn", [128, 1024], F32)
        attnT = S2("attnT", [128, 4 * 1024], BF16)
        x2 = S2("x2", [128, 2 * 1024], F32)
        g2 = S2("g2", [128, 2 * 1024], F32)
        sg = S2("sg", [128, 512], F32)
        outb = S2("outb", [128, 1024], F32)

        for k, src in (("wq", wT_d["wqT"]), ("wk", wT_d["wkT"]),
                       ("wv", wT_d["wvT"]), ("wo", wT_d["woT"])):
            E("gpsimd", lambda e, k=k, src=src: e.dma_start(
                wbf[k].ap().rearrange("p (c o) -> p c o", c=8),
                src.ap().rearrange("(c p) o -> p c o", p=128)),
              inc="SG", incn=16)
        sg_w = V("SG")
        E("vector", lambda e: e.memset(vp[:], 1.0), inc="DV")
        dv_vp0 = V("DV")

        xnT2_g = 0
        pb01f = [0, 0]
        pb23f = [("DV", 0), ("DV", 0)]
        zr_free = [0, 0]
        attn_free = 0
        pb3f = 0
        pb6f = 0
        pb7f = 0
        pb45f = 0
        sq_g = [0, 0]
        nhl_free = 0
        x2f = [0, 0]
        g2f = [0, 0]
        outb_st = [0]
        ssb_ev = 0

        for s in range(SEQ):
            E("vector", lambda e: e.memset(kvb[0:64, 0:1024], 0.0), inc="DV")
            E("vector", lambda e: e.memset(dnb[0:64, 0:16], 0.0), inc="DV")
            for qtr in range(4):
                pb23f = [("DV", V("DV")), ("DV", V("DV"))]
                pb45f = V("DV")
                pb2f = V("DV")
                dvsnap = V("DV")
                for T in range(1):
                    tg0 = qtr * 512
                    E("sync", lambda e, s=s, tg0=tg0: e.dma_start(
                        xnT2.ap().rearrange("p (c t) -> p c t", c=8),
                        xnT_s.ap()[s].rearrange("c p t -> p c t")[
                            :, :, tg0:tg0 + 512]),
                      waits=[("PE", xnT2_g)], inc="SX", incn=16)
                    sx_x = V("SX")
                    hilo_dv = [0, 0]
                    dv_scan = V("DV")
                    phb_free = [0, 0]
                    for idx, wname in ((0, "wq"), (1, "wk")):
                        qk = qT if idx == 0 else kT
                        prev = None
                        for p in range(9):
                            if p < 8:
                                for Dc in range(8):
                                    w = []
                                    if Dc == 0:
                                        w = [("SX", sx_x), ("SG", sg_w),
                                             ("AC", pb01f[p % 2])]
                                        if p == 0 and idx == 0 and T == 0:
                                            w.append(("DV", dvsnap))
                                    E("tensor", lambda e, wname=wname, p=p,
                                      Dc=Dc: e.matmul(
                                          PB[p % 2][:, 0:512],
                                          wbf[wname][:,
                                                     Dc * 1024 + p * 128:
                                                     Dc * 1024 +
                                                     (p + 1) * 128],
                                          xnT2[:, Dc * 512:(Dc + 1) * 512],
                                          start=(Dc == 0), stop=(Dc == 7),
                                          skip_group_check=True),
                                      waits=w,
                                      inc=("PE" if Dc == 7 else None))
                                pe_q = V("PE")
                            if p >= 1:
                                pm = p - 1
                                sl = pm % 2
                                dv_sq, ac_qkm = prev
                                if idx == 0:
                                    w = [("SG", sg_setup), ("AC", ac_qkm),
                                         ("AC", phb_free[pm % 2])]
                                    if pm < 2:
                                        w.append(("DV", dv_scan))
                                    E("tensor", lambda e, pm=pm: e.matmul(
                                        PB[2 + pm % 2][:, 0:512],
                                        ombd_sb[:],
                                        qT[:, pm * 512:(pm + 1) * 512],
                                        start=True, stop=False,
                                        skip_group_check=True),
                                      waits=w)
                                    E("tensor", lambda e, pm=pm, sl=sl:
                                      e.matmul(
                                          PB[2 + pm % 2][:, 0:512],
                                          hsel_sb[:],
                                          sqp[:, sl * 512:(sl + 1) * 512],
                                          start=False, stop=True,
                                          skip_group_check=True),
                                      waits=[("DV", dv_sq),
                                             ("SX", sx_setup)],
                                      inc="PE")
                                    sq_g[sl] = V("PE")
                                    pe_phi = V("PE")
                                    E("scalar", lambda e, pm=pm:
                                      e.activation(
                                          phiqT[:, pm * 512:
                                                (pm + 1) * 512],
                                          PB[2 + pm % 2][:, 0:512],
                                          AF.Exp,
                                          bias=negl8_sb[:, 0:1]),
                                      waits=[("PE", pe_phi),
                                             ("DV", dv_ones)],
                                      inc="AC")
                                    E("scalar", lambda e, pm=pm:
                                      e.activation(
                                          phiqTo[:, pm * 512:
                                                 (pm + 1) * 512],
                                          PB[2 + pm % 2][64:128, 0:512],
                                          AF.Exp,
                                          bias=negl8_sb[0:64, 0:1]),
                                      inc="AC")
                                    phb_free[pm % 2] = V("AC")
                                else:
                                    for ts in range(4):
                                        w = [("DV", dv_sq)]
                                        if pm == 0 and ts == 0:
                                            w += [("DV", pb2f),
                                                  ("AC", phb_free[0])]
                                        E("tensor", lambda e, pm=pm, ts=ts,
                                          sl=sl: e.matmul(
                                              PB[2][:, 64 + ts * 16:
                                                    64 + (ts + 1) * 16],
                                              sqp[:, sl * 512 + ts * 128:
                                                  sl * 512 +
                                                  (ts + 1) * 128],
                                              seln_sb[:, pm * 16:
                                                      (pm + 1) * 16],
                                              start=(pm == 0 and ts == 0),
                                              stop=(pm == 7 and ts == 3),
                                              skip_group_check=True),
                                          waits=w, inc="PE")
                                    sq_g[sl] = V("PE")
                            if p < 8:
                                E("scalar", lambda e, qk=qk, p=p:
                                  e.activation(
                                      qk[:, p * 512:(p + 1) * 512],
                                      PB[p % 2][:, 0:512], AF.Copy),
                                  waits=[("PE", pe_q)], inc="AC")
                                ac_qk = V("AC")
                                if idx == 1:
                                    E("scalar", lambda e, p=p:
                                      e.activation(
                                          kTo[:, p * 512:(p + 1) * 512],
                                          PB[p % 2][64:128, 0:512],
                                          AF.Copy),
                                      inc="AC")
                                pb01f[p % 2] = V("AC")
                                E("vector", lambda e, qk=qk, p=p:
                                  e.tensor_tensor(
                                      sqp[:, (p % 2) * 512:
                                          (p % 2 + 1) * 512],
                                      qk[:, p * 512:(p + 1) * 512],
                                      qk[:, p * 512:(p + 1) * 512],
                                      ALU.mult),
                                  waits=[("AC", ac_qk),
                                         ("PE", sq_g[p % 2])],
                                  inc="DV")
                                prev = (V("DV"), ac_qk)
                        if idx == 0:
                            continue
                        pe_norm = V("PE")
                        for ts in range(4):
                            off = 64 + ts * 16
                            E("vector", lambda e, off=off: e.tensor_copy(
                                nhb[:], PB[2][:, off:off + 16]),
                              waits=[("PE", pe_norm), ("DV", V("DV")),
                                     ("PE", nhl_free)],
                              inc="DV")
                            E("vector", lambda e: e.tensor_copy(
                                nhl[:, 0:16], nhb[:]),
                              waits=[("DV", V("DV"))], inc="DV")
                            E("vector", lambda e, off=off: e.tensor_tensor(
                                nhl[:, 16:32], PB[2][:, off:off + 16],
                                nhl[:, 0:16], ALU.subtract),
                              waits=[("DV", V("DV"))], inc="DV")
                            dv_nhl = V("DV")
                            E("tensor", lambda e, ts=ts: e.transpose(
                                PB[3][0:32, ts * 128:(ts + 1) * 128],
                                nhl[:], ident_sb[:]),
                              waits=[("DV", dv_nhl), ("DV", pb3f)],
                              inc="PE")
                            pe_ntp = V("PE")
                            nhl_free = pe_ntp
                            E("vector", lambda e, ts=ts:
                              e.tensor_copy(
                                  nrTk[0:32, ts * 128:(ts + 1) * 128],
                                  PB[3][0:32, ts * 128:(ts + 1) * 128]),
                              waits=[("PE", pe_ntp)], inc="DV")
                            pb3f = V("DV")
                        hilo_dv[1] = V("DV")
                        pb2f = V("DV")
                    if STAGE < 3:
                        continue
                    for p in range(8):
                        w = [("SG", sg_setup), ("DV", hilo_dv[1]),
                             ("AC", pb01f[p % 2])]
                        E("tensor", lambda e, p=p: e.matmul(
                            PB[p % 2][:, 0:512], ombd_sb[:],
                            kT[:, p * 512:(p + 1) * 512],
                            start=True, stop=False,
                            skip_group_check=True),
                          waits=w)
                        E("tensor", lambda e, p=p: e.matmul(
                            PB[p % 2][:, 0:512],
                            shF_sb[:, p * 128:(p + 1) * 128],
                            nrTk[0:32, :], start=False, stop=True,
                            skip_group_check=True),
                          inc="PE")
                        pe_phi = V("PE")
                        E("scalar", lambda e, p=p:
                          e.activation(
                              phikT[:, p * 512:(p + 1) * 512],
                              PB[p % 2][:, 0:512], AF.Exp,
                              bias=negl8_sb[:, 0:1]),
                          waits=[("PE", pe_phi), ("DV", dv_ones)],
                          inc="AC")
                        E("scalar", lambda e, p=p:
                          e.activation(
                              phikTo[:, p * 512:(p + 1) * 512],
                              PB[p % 2][64:128, 0:512], AF.Exp,
                              bias=negl8_sb[0:64, 0:1]),
                          inc="AC")
                        pb01f[p % 2] = V("AC")
                    if STAGE < 4:
                        xnT2_g = 0
                        continue
                    for ts in range(4):
                        for oh in range(2):
                            for Dc in range(8):
                                w = []
                                if Dc == 0 and oh == 0:
                                    w = [("DV", pb45f), ("SG", sg_w)]
                                E("tensor", lambda e, ts=ts, oh=oh, Dc=Dc:
                                  e.matmul(
                                      PB[4 + oh][:, 0:512],
                                      xnT2[:, Dc * 512 + ts * 128:
                                           Dc * 512 + (ts + 1) * 128],
                                      wbf["wv"][:, Dc * 1024 + oh * 512:
                                                Dc * 1024 +
                                                (oh + 1) * 512],
                                      start=(Dc == 0), stop=(Dc == 7),
                                      skip_group_check=True),
                                  waits=w,
                                  inc=("PE" if Dc == 7 else None))
                        pe_v = V("PE")
                        cl = T * 4 + ts
                        for bank in range(2):
                            dst = vp.ap().rearrange(
                                "p (cc h j) -> p cc h j", cc=4, h=16)[
                                :, cl, bank * 8:(bank + 1) * 8, 0:64]
                            src = PB[4 + bank].ap().rearrange(
                                "p (h j) -> p h j", h=8)
                            E("vector", lambda e, dst=dst, src=src:
                              e.tensor_copy(dst, src),
                              waits=[("PE", pe_v), ("DV", dv_vp0)],
                              inc="DV")
                        pb45f = V("DV")
                    xnT2_g = V("PE")
                    if STAGE < 5:
                        continue
                    pb67_ac = V("AC")
                    for ts in range(4):
                        cl = T * 4 + ts
                        ptk_order = [0, 2, 4, 6, 8, 10, 12, 14,
                                     1, 3, 5, 7, 9, 11, 13, 15]
                        for oi, hh in enumerate(ptk_order):
                            bank = 6 + hh // 8
                            kk = kT if hh % 2 == 0 else kTo
                            w = []
                            if oi == 0:
                                w = [("AC", pb67_ac)]
                            E("tensor", lambda e, hh=hh, bank=bank,
                              kk=kk, ts=ts: e.matmul(
                                  PB[bank][:, (hh % 8) * 64:
                                           (hh % 8) * 64 + 64],
                                  kk[0:64,
                                     (hh // 2) * 512 + ts * 128:
                                     (hh // 2) * 512 +
                                     (ts + 1) * 128],
                                  ombd_sb[0:64, 0:64],
                                  start=(hh == 0 or hh == 8), stop=False,
                                  skip_group_check=True),
                              waits=w, inc="PE")
                        for bank in range(2):
                            E("tensor", lambda e, ts=ts, bank=bank:
                              e.matmul(
                                  PB[bank + 6][:, 0:512],
                                  nrTk[:, ts * 128:(ts + 1) * 128],
                                  shT_sb[:, bank * 512:(bank + 1) * 512],
                                  start=False, stop=True,
                                  skip_group_check=True),
                              waits=([("DV", hilo_dv[1])]
                                     if bank == 0 else []),
                              inc="PE")
                        pe_pt = V("PE")
                        if DEBUG and s == 0 and qtr == 0 and ts == 0:
                            E("gpsimd", lambda e: e.dma_start(
                                dbg_nr.ap(), nrTk[:]),
                              waits=[("DV", hilo_dv[1])],
                              inc="SG", incn=16)
                        for bank in range(2):
                            E("scalar", lambda e, cl=cl, bank=bank:
                              e.activation(
                                  phikt[:, cl * 1024 + bank * 512:
                                        cl * 1024 + (bank + 1) * 512],
                                  PB[bank + 6][:, 0:512], AF.Exp,
                                  bias=negl8_sb[:, 0:1]),
                              waits=[("PE", pe_pt)], inc="AC")
                        pb67_ac = V("AC")
                ac_phi = V("AC")
                if STAGE < 6:
                    continue
                if DEBUG:
                    E("gpsimd", lambda e, s=s, qtr=qtr: e.dma_start(
                        dbg_pt.ap()[s * 4 + qtr], phikt[:]),
                      waits=[("AC", ac_phi)], inc="SG", incn=16)

                pb01s = V("DV")
                pb23f = [("DV", V("DV")), ("DV", V("DV"))]
                pb45f = V("DV")
                pb6f = V("DV")
                pb7f = V("DV")
                for cl in range(4):
                    c = qtr * 4 + cl
                    slotc = cl % 2
                    dvm = [0, 0]
                    for w2_ in range(2):
                        for hh in range(8):
                            h = w2_ * 8 + hh
                            pk_ = phikT if h % 2 == 0 else phikTo
                            pq_ = phiqT if h % 2 == 0 else phiqTo
                            pcol = h // 2
                            w = []
                            if hh == 0:
                                w = [("AC", ac_phi), ("DV", pb01s)]
                            E("tensor", lambda e, pk_=pk_, pq_=pq_,
                              pcol=pcol, hh=hh, cl=cl: e.matmul(
                                  PB[hh % 2][:, (hh // 2) * 128:
                                             (hh // 2) * 128 + 128],
                                  pk_[0:64,
                                      pcol * 512 + cl * 128:
                                      pcol * 512 + (cl + 1) * 128],
                                  pq_[0:64,
                                      pcol * 512 + cl * 128:
                                      pcol * 512 + (cl + 1) * 128],
                                  start=(hh < 2), stop=(hh >= 6),
                                  skip_group_check=True),
                              waits=w, inc="PE")
                        pe_w = V("PE")
                        for par in range(2):
                            d0 = (w2_ * 8 + par * 4) * 128
                            E("vector", lambda e, par=par, d0=d0:
                              e.tensor_tensor(
                                  BT[:, d0:d0 + 512], PB[par][:, 0:512],
                                  mask_sb[:], ALU.mult),
                              waits=[("PE", pe_w)], inc="DV")
                        pb01s = V("DV")
                        dvm[w2_] = V("DV")
                    for p in range(8):
                        nb = 2 + p % 2
                        fw = [("DV", dvm[p // 4]), ("DV", ssb_ev),
                              pb23f[p % 2]]
                        if c > 0:
                            E("tensor", lambda e, p=p, nb=nb, cl=cl:
                              e.matmul(
                                  PB[nb][:, 0:65],
                                  phiqT[0:64, p * 512 + cl * 128:
                                        p * 512 + (cl + 1) * 128],
                                  Ssb[:, (2 * p) * 65:(2 * p + 1) * 65],
                                  start=True, stop=False,
                                  skip_group_check=True),
                              waits=fw)
                            fw = []
                        E("tensor", lambda e, p=p, nb=nb, cl=cl, c=c:
                          e.matmul(
                              PB[nb][:, 0:65],
                              BT[:, bcol(2 * p) * 128:
                                 (bcol(2 * p) + 1) * 128],
                              vp[:, cl * 1040 + (2 * p) * 65:
                                 cl * 1040 + (2 * p) * 65 + 65],
                              start=(c == 0), stop=False,
                              skip_group_check=True),
                          waits=fw, inc="PE")
                        if c > 0:
                            E("tensor", lambda e, p=p, nb=nb, cl=cl:
                              e.matmul(
                                  PB[nb][:, 65:130],
                                  phiqTo[0:64, p * 512 + cl * 128:
                                         p * 512 + (cl + 1) * 128],
                                  Ssb[:, (2 * p + 1) * 65:
                                      (2 * p + 2) * 65],
                                  start=False, stop=False,
                                  skip_group_check=True))
                        E("tensor", lambda e, p=p, nb=nb, cl=cl, c=c:
                          e.matmul(
                              PB[nb][:, 65:130],
                              BT[:, bcol(2 * p + 1) * 128:
                                 (bcol(2 * p + 1) + 1) * 128],
                              vp[:, cl * 1040 + (2 * p + 1) * 65:
                                 cl * 1040 + (2 * p + 1) * 65 + 65],
                              start=False, stop=True,
                              skip_group_check=True),
                          inc="PE")
                        pe_num = V("PE")
                        zq = (p % 2) * 4
                        E("vector", lambda e, nb=nb, zq=zq:
                          e.tensor_scalar(
                            zr[:, zq:zq + 2], PB[nb][:, 64:130:65], 1e-6,
                            None, ALU.add),
                          waits=[("PE", pe_num), ("AC", zr_free[p % 2])],
                          inc="DV")
                        E("vector", lambda e, zq=zq: e.reciprocal(
                            zr[:, zq + 2:zq + 4], zr[:, zq:zq + 2]),
                          waits=[("DV", V("DV"))], inc="DV")
                        dv_z = V("DV")
                        w_at = [("DV", dv_z)]
                        if p == 0:
                            w_at.append(("PE", attn_free))
                        E("scalar", lambda e, nb=nb, p=p, zq=zq:
                          e.activation(
                              attn[:, p * 128:p * 128 + 64],
                              PB[nb][:, 0:64], AF.Copy,
                              scale=zr[:, zq + 2:zq + 3]),
                          waits=w_at, inc="AC")
                        E("scalar", lambda e, nb=nb, p=p, zq=zq:
                          e.activation(
                              attn[:, p * 128 + 64:(p + 1) * 128],
                              PB[nb][:, 65:129], AF.Copy,
                              scale=zr[:, zq + 3:zq + 4]),
                          inc="AC")
                        pb23f[p % 2] = ("AC", V("AC"))
                        zr_free[p % 2] = V("AC")
                    dv_attn = V("AC")
                    for h in range(16):
                        sb_ = 4 + h // 8
                        w = []
                        if h == 0:
                            w = [("DV", ssb_ev), ("AC", ac_phi),
                                 ("DV", pb45f)]
                        E("tensor", lambda e, h=h, sb_=sb_, cl=cl:
                          e.matmul(
                              PB[sb_][0:64,
                                      (h % 8) * 64:(h % 8) * 64 + 64],
                              phikt[:, cl * 1024 + h * 64:
                                    cl * 1024 + (h + 1) * 64],
                              vp[:, cl * 1040 + h * 65:
                                 cl * 1040 + h * 65 + 64],
                              start=(cl == 0 and h % 8 == 0),
                              stop=(cl == 3), skip_group_check=True),
                          waits=w, inc="PE")
                    for h in range(16):
                        w = []
                        if h == 0 and cl == 0:
                            w = [("DV", pb7f)]
                        E("tensor", lambda e, h=h, cl=cl: e.matmul(
                            PB[7][0:64, h:h + 1],
                            phikt[:, cl * 1024 + h * 64:
                                  cl * 1024 + (h + 1) * 64],
                            vp[:, cl * 1040 + h * 65 + 64:
                               cl * 1040 + h * 65 + 65],
                            start=(cl == 0 and h == 0), stop=(cl == 3),
                            skip_group_check=True),
                          waits=w, inc="PE")
                    pe_su = V("PE")
                    qp = qtr % 2
                    for bank in range(2):
                        dst = Ssb.ap().rearrange(
                            "p (h j) -> p h j", h=16)[
                            :, bank * 8:(bank + 1) * 8, 0:64]
                        E("vector", lambda e, bank=bank, qp=qp, dst=dst:
                          e.tensor_tensor(
                              dst, PB[4 + bank][0:64, 0:512],
                              kvb.ap().rearrange(
                                  "p (b n) -> p b n", b=2)[
                                  :, qp, bank * 512:(bank + 1) * 512],
                              ALU.add),
                          waits=[("PE", pe_su)], inc="DV")
                    dstd = Ssb.ap().rearrange("p (h j) -> p h j", h=16)[
                        :, :, 64:65]
                    E("vector", lambda e, qp=qp, dstd=dstd:
                      e.tensor_tensor(
                          dstd,
                          PB[7].ap()[0:64, 0:16].rearrange(
                              "p (h o) -> p h o", h=16),
                          dnb.ap()[0:64, qp * 16:(qp + 1) * 16].rearrange(
                              "p (h o) -> p h o", h=16),
                          ALU.add),
                      waits=[("PE", pe_su)], inc="DV")
                    if cl == 3:
                        E("vector", lambda e, qp=qp: e.tensor_tensor(
                            kvb[0:64, (1 - qp) * 1024:
                                (1 - qp) * 1024 + 512],
                            PB[4][0:64, 0:512],
                            kvb[0:64, qp * 1024:qp * 1024 + 512],
                            ALU.add),
                          waits=[("DV", V("DV"))], inc="DV")
                        E("vector", lambda e, qp=qp: e.tensor_tensor(
                            kvb[0:64, (1 - qp) * 1024 + 512:
                                (1 - qp) * 1024 + 1024],
                            PB[5][0:64, 0:512],
                            kvb[0:64, qp * 1024 + 512:(qp + 1) * 1024],
                            ALU.add),
                          inc="DV")
                        E("vector", lambda e, qp=qp: e.tensor_tensor(
                            dnb[0:64, (1 - qp) * 16:(2 - qp) * 16],
                            PB[7][0:64, 0:16],
                            dnb[0:64, qp * 16:(qp + 1) * 16],
                            ALU.add),
                          inc="DV")
                    ssb_ev = V("DV")
                    pb45f = ssb_ev
                    if DEBUG:
                        E("gpsimd", lambda e, s=s, qtr=qtr, cl=cl:
                          e.dma_start(
                              dbg_ss.ap()[(s * 4 + qtr) * 4 + cl],
                              Ssb[:]),
                          waits=[("DV", ssb_ev)], inc="SG", incn=16)
                    gi = cl % 4
                    for w2_ in range(2):
                        for j in range(4):
                            w = ([("AC", dv_attn), ("DV", pb6f)]
                                 if j == 0 else [])
                            E("tensor", lambda e, w2_=w2_,
                              j=j: e.transpose(
                                  PB[6][:, j * 128:(j + 1) * 128],
                                  attn[:, (w2_ * 4 + j) * 128:
                                       (w2_ * 4 + j) * 128 + 128],
                                  ident_sb[:]),
                              waits=w, inc="PE")
                        pe_tr = V("PE")
                        attn_free = pe_tr
                        dst = attnT.ap().rearrange(
                            "p (gi ic c) -> p gi ic c", gi=4, ic=8)[
                            :, gi, w2_ * 4:(w2_ + 1) * 4, :]
                        src = PB[6].ap().rearrange(
                            "p (ic c) -> p ic c", ic=4)
                        E("vector", lambda e, dst=dst, src=src:
                          e.tensor_copy(dst, src),
                          waits=[("PE", pe_tr)], inc="DV")
                        pb6f = V("DV")
                    attnT_rdy = V("DV")
                    if STAGE >= 7 and cl % 4 == 3:
                        for gi2 in range(4):
                            tg = qtr * 512 + (cl - 3 + gi2) * 128
                            xs = gi2 % 2
                            E("gpsimd", lambda e, s=s, tg=tg, xs=xs:
                              e.dma_start(
                                  x2[:, xs * 1024:(xs + 1) * 1024],
                                  x_d.ap()[s, tg:tg + 128, :]),
                              waits=[("DV", x2f[xs])], inc="SG", incn=16)
                            E("gpsimd", lambda e, s=s, tg=tg, xs=xs:
                              e.dma_start(
                                  g2[:, xs * 1024:(xs + 1) * 1024],
                                  g_s.ap()[s, tg:tg + 128, :]),
                              waits=[("DV", g2f[xs])], inc="SG", incn=16)
                            sx_c = V("SG")
                            os_ = 0
                            for eh in range(2):
                                for ic in range(8):
                                    w = []
                                    if ic == 0:
                                        w = [("DV", attnT_rdy),
                                             ("DV", pb7f)]
                                    E("tensor", lambda e, gi2=gi2, ic=ic,
                                      eh=eh: e.matmul(
                                          PB[7][:, 0:512],
                                          attnT[:, gi2 * 1024 +
                                                ic * 128:
                                                gi2 * 1024 +
                                                (ic + 1) * 128],
                                          wbf["wo"][:, ic * 1024 +
                                                    eh * 512:
                                                    ic * 1024 +
                                                    (eh + 1) * 512],
                                          start=(ic == 0), stop=False,
                                          skip_group_check=True),
                                      waits=w)
                                E("tensor", lambda e, eh=eh: e.matmul(
                                    PB[7][:, 0:512], ones2_sb[:],
                                    bohl_sb[:, eh * 512:(eh + 1) * 512],
                                    start=False, stop=True,
                                    skip_group_check=True),
                                  inc="PE")
                                pe_wo = V("PE")
                                E("vector", lambda e, xs=xs, eh=eh:
                                  e.tensor_tensor(
                                      sg[:], PB[7][:, 0:512],
                                      g2[:, xs * 1024 + eh * 512:
                                         xs * 1024 + (eh + 1) * 512],
                                      ALU.mult),
                                  waits=[("PE", pe_wo), ("SG", sx_c)],
                                  inc="DV")
                                E("vector", lambda e, xs=xs, eh=eh,
                                  os_=os_: e.tensor_tensor(
                                      outb[:, os_ * 1024 + eh * 512:
                                           os_ * 1024 +
                                           (eh + 1) * 512],
                                      sg[:],
                                      x2[:, xs * 1024 + eh * 512:
                                         xs * 1024 + (eh + 1) * 512],
                                      ALU.add),
                                  waits=[("DV", V("DV")),
                                         ("SC", outb_st[os_])],
                                  inc="DV")
                                pb7f = V("DV")
                            dv_out = V("DV")
                            E("scalar", lambda e, s=s, tg=tg, os_=os_:
                              e.dma_start(
                                  y_d.ap()[s, tg:tg + 128, :],
                                  outb[:, os_ * 1024:(os_ + 1) * 1024]),
                              waits=[("DV", dv_out)], inc="SC", incn=16)
                            outb_st[os_] = V("SC")
                            x2f[xs] = dv_out
                            g2f[xs] = dv_out

        if DEBUG:
            prog.barrier()
            E("sync", lambda e: e.dma_start(dbg_xnt.ap(), xnT_s.ap()),
              inc="SX", incn=16)
            E("sync", lambda e: e.dma_start(dbg_g.ap(), g_s.ap()),
              inc="SX", incn=16)
        prog.barrier()
        prog.build()

    return nc


def _host_consts():
    ident = np.eye(128, dtype=np.float32)
    mask = np.ascontiguousarray(
        np.tile(np.triu(np.ones((128, 128), dtype=np.float32)), (1, 4)))
    seln = np.zeros((128, 128), dtype=np.float32)
    for p in range(8):
        seln[0:64, p * 16 + 2 * p] = -0.5
        seln[64:128, p * 16 + 2 * p + 1] = -0.5
    shF = np.zeros((32, 1024), dtype=np.float32)
    for j in range(32):
        hsel = j % 16
        p = hsel // 2
        if hsel % 2 == 0:
            shF[j, p * 128:p * 128 + 64] = 1.0
        else:
            shF[j, p * 128 + 64:(p + 1) * 128] = 1.0
    shT = np.zeros((32, 1024), dtype=np.float32)
    for j in range(32):
        shT[j, (j % 16) * 64:((j % 16) + 1) * 64] = 1.0
    shFo = np.zeros((128, 512), dtype=np.float32)
    for j in range(32):
        hsel = j % 16
        if hsel % 2 == 1:
            p = hsel // 2
            shFo[j, p * 64:(p + 1) * 64] = 1.0
    hsel = np.zeros((128, 128), dtype=np.float32)
    hsel[0:64, 0:64] = -0.5
    hsel[64:128, 64:128] = -0.5
    return ident, mask, seln, shF, shT, shFo, hsel


def kernel(L0_attn_x, L0_attn_omega, L0_attn_wq, L0_attn_wk, L0_attn_wv,
           L0_attn_wo, L0_attn_bo, L0_ffn_W1, L0_ffn_b1, L0_ffn_W2,
           L0_ffn_b2):
    x = np.asarray(L0_attn_x, dtype=np.float32)
    ident, mask, seln, shF, shT, shFo, hsel = _host_consts()
    omT = np.ascontiguousarray(np.asarray(L0_attn_omega, np.float32).T)
    ombd = np.zeros((128, 128), dtype=np.float32)
    ombd[0:64, 0:64] = omT
    ombd[64:128, 64:128] = omT
    common = dict(
        wqT=np.ascontiguousarray(np.asarray(L0_attn_wq, np.float32).T),
        wkT=np.ascontiguousarray(np.asarray(L0_attn_wk, np.float32).T),
        wvT=np.ascontiguousarray(np.asarray(L0_attn_wv, np.float32).T),
        woT=np.ascontiguousarray(np.asarray(L0_attn_wo, np.float32).T),
        ombd=ombd,
        w1T=np.ascontiguousarray(np.asarray(L0_ffn_W1, np.float32).T),
        w2T=np.ascontiguousarray(np.asarray(L0_ffn_W2, np.float32).T),
        b1c=np.ascontiguousarray(
            np.asarray(L0_ffn_b1, np.float32).reshape(32, 128).T),
        b2c=np.ascontiguousarray(
            np.asarray(L0_ffn_b2, np.float32).reshape(8, 128).T),
        bo_row=np.asarray(L0_attn_bo, np.float32).reshape(1, D),
        ident=ident, mask4=mask, seln=seln, selhlF=shF, selhlT=shT,
        halfsel=hsel,
    )
    nc = build_nc()
    in_maps = []
    for c in range(NCORES):
        m = dict(common)
        m["x"] = np.ascontiguousarray(x[2 * c:2 * c + 2])
        in_maps.append(m)
    res = run_bass_kernel_spmd(nc, in_maps, core_ids=list(range(NCORES)))
    out = np.concatenate([r["y"].reshape(SEQ, N, D) for r in res.results],
                         axis=0)
    return np.ascontiguousarray(out, dtype=np.float32)
